# version 05 (final)
# baseline (speedup 1.0000x reference)
import numpy as np
from contextlib import ExitStack

import concourse.bass as bass
import concourse.mybir as mybir
from concourse.bass_utils import run_bass_kernel_spmd
from concourse.alu_op_type import AluOpType as ALU

F32 = mybir.dt.float32
BF16 = mybir.dt.bfloat16
AF = mybir.ActivationFunctionType
AX = mybir.AxisListType.X

DEBUG_TAPS = False
NO_CC = False
STRICT_ALL = False
PHASES = {"ph0", "s1", "s2", "proj"}

D = 1024
DI = 2048
DST = 64
NH = 32
HD = 64
CV = 2176
DIP = 4256
R = 4096
RXP = 4608
Q = 128
NT = 8
T = 512
NPT = 9
EPS = 1e-5
NEGBIG = float(2 ** 20)

FH = 8
FD = 64
FI = 512
WIN = 512
HID = 2730
HIDP = 2752
H2P = 2 * HIDP
QKS = 8.0


class Op:
    __slots__ = ("eng", "fn", "deps", "inc_val", "needs_inc", "is_dma", "is_cc", "ccsem", "strict")

    def __init__(self, eng, fn, deps, is_dma, is_cc=False):
        self.strict = False
        self.eng = eng
        self.fn = fn
        self.deps = deps
        self.inc_val = None
        self.needs_inc = False
        self.is_dma = is_dma
        self.is_cc = is_cc
        self.ccsem = None


class Buf:
    __slots__ = ("writers", "readers", "name")

    def __init__(self, name=""):
        self.writers = []
        self.readers = []
        self.name = name


class Sched:
    ENGS = ("pe", "act", "dve", "pool", "sp")

    def __init__(self):
        self.ops = {e: [] for e in self.ENGS}
        self.all_ops = []
        self.bar = {e: [] for e in self.ENGS}

    def _mk(self, eng, fn, deps, is_dma, is_cc=False):
        deps = [d for d in deps if d is not None]
        if self.bar[eng]:
            deps = deps + self.bar[eng]
            self.bar[eng] = []
        o = Op(eng, fn, deps, is_dma, is_cc)
        self.ops[eng].append(o)
        self.all_ops.append(o)
        return o

    def do(self, eng, fn, reads=(), writes=(), extra=(), is_dma=False, is_cc=False, strict=False):
        deps = list(extra)
        for b in reads:
            deps.extend(b.writers)
        for b in writes:
            deps.extend(b.readers)
            for w in b.writers:
                if w.eng != eng or w.is_dma != is_dma:
                    deps.append(w)
        seen = set()
        dd = []
        for d in deps:
            if d is not None and id(d) not in seen:
                seen.add(id(d))
                dd.append(d)
        o = self._mk(eng, fn, dd, is_dma, is_cc)
        o.strict = strict or STRICT_ALL
        for b in reads:
            b.readers.append(o)
        for b in writes:
            b.writers.append(o)
            b.readers = []
        return o

    def dma(self, eng, out, in_, reads=(), writes=(), extra=(), slow=False):
        def fn(h):
            return h.dma_start(out=out, in_=in_, allow_slow_non_contiguous=slow)
        return self.do(eng, fn, reads, writes, extra, is_dma=True)

    def barrier(self):
        lasts = [self.ops[e][-1] for e in self.ENGS if self.ops[e]]
        for e in self.ENGS:
            self.bar[e] = list(lasts)

    def emit(self, nc, ctx):
        for o in self.all_ops:
            for d in o.deps:
                if d.eng != o.eng or d.is_dma != o.is_dma or (d.eng == o.eng and not d.is_dma):
                    d.needs_inc = True
        sems, dsems = {}, {}
        for e in self.ENGS:
            sems[e] = ctx.enter_context(nc.semaphore(f"s_{e}"))
            dsems[e] = ctx.enter_context(nc.semaphore(f"d_{e}"))
        cnt = {e: 0 for e in self.ENGS}
        dcnt = {e: 0 for e in self.ENGS}
        ccsems = []
        NDS = {"pool": 64, "sp": 8, "act": 4, "dve": 4, "pe": 2}
        dsem_pool = {e: [ctx.enter_context(nc.semaphore(f"dp_{e}{i}")) for i in range(NDS[e])]
                     for e in self.ENGS}
        dsem_cnt = {e: [0] * NDS[e] for e in self.ENGS}
        dsem_rr = {e: 0 for e in self.ENGS}
        for e in self.ENGS:
            for o in self.ops[e]:
                if o.is_cc:
                    o.ccsem = ctx.enter_context(nc.semaphore(f"cc{len(ccsems)}"))
                    ccsems.append(o.ccsem)
                    o.inc_val = ("cc", o, 1)
                elif o.is_dma:
                    si = dsem_rr[e]
                    dsem_rr[e] = (si + 1) % NDS[e]
                    dsem_cnt[e][si] += 16
                    o.inc_val = ("d", (e, si), dsem_cnt[e][si])
                elif o.needs_inc:
                    cnt[e] += 1
                    o.inc_val = ("c", e, cnt[e])
        self.dsem_pool = dsem_pool
        block = ctx.enter_context(nc.Block())
        sections = {"pe": block.tensor, "act": block.scalar, "dve": block.vector,
                    "pool": block.gpsimd, "sp": block.sync}

        def mk(e):
            def section(h):
                waited = {}
                for o in self.ops[e]:
                    for d in o.deps:
                        if d.is_cc:
                            key = ("cc", id(d))
                            if waited.get(key, -1) >= 1:
                                continue
                            waited[key] = 1
                            h.wait_ge(d.ccsem, 1)
                            continue
                        if d.eng == e and not d.is_dma and e == "pe" and not o.strict:
                            continue
                        if d.eng == e and d.is_dma and o.is_dma:
                            continue
                        if d.inc_val is None:
                            continue
                        kind, de, v = d.inc_val
                        key = (kind, de)
                        if waited.get(key, -1) >= v:
                            continue
                        waited[key] = v
                        if kind == "d":
                            h.wait_ge(self.dsem_pool[de[0]][de[1]], v)
                        else:
                            h.wait_ge(sems[de], v)
                    instr = o.fn(h)
                    if o.inc_val is not None:
                        kind, de, v = o.inc_val
                        if instr is None:
                            instr = h.nop()
                        if kind == "cc":
                            instr.then_inc(o.ccsem)
                        elif kind == "d":
                            instr.then_inc(self.dsem_pool[de[0]][de[1]], 16)
                        else:
                            instr.then_inc(sems[de], 1)
            return section

        for e in self.ENGS:
            sections[e](mk(e))


def host_consts():
    c = {}
    i = np.arange(128)
    c["ident_f32"] = np.eye(128, dtype=np.float32)
    c["negbig_ident"] = (-NEGBIG) * np.eye(128, dtype=np.float32)
    c["maskrep_fwd"] = np.tile((i[:, None] > i[None, :]).astype(np.float32), (1, 8))
    c["maskrep_bwd"] = np.tile((i[:, None] < i[None, :]).astype(np.float32), (1, 8))
    c["amask_diag"] = (i[None, :] > i[:, None]).astype(np.float32)
    c["amask_far"] = (i[None, :] < i[:, None]).astype(np.float32)
    dp = np.zeros((32, 4096), dtype=np.float32)
    for h in range(32):
        dp[h, h * 128:(h + 1) * 128] = 1.0
    c["deltapat"] = np.tile(dp, (3, 1))
    c["ones3"] = np.ones((3, 128), dtype=np.float32)
    c["negones3"] = -np.ones((3, 128), dtype=np.float32)
    c["ones1x128"] = np.ones((1, 128), dtype=np.float32)
    c["onescol"] = np.ones((128, 1), dtype=np.float32)
    op_ = np.zeros((128, 2), dtype=np.float32)
    op_[:64, 0] = 1.0
    op_[64:, 1] = 1.0
    c["onespair"] = op_
    c["onespairT"] = op_.T.copy()
    c["iota512"] = np.arange(512, dtype=np.float32)[None, :]
    half = np.arange(0, FD, 2, dtype=np.float32)
    invf = 1.0 / 10000.0 ** (half / FD)
    c["invf_row"] = np.concatenate([invf, invf])[None, :].astype(np.float32)
    base = (half + 0.4 * FD) / (1.4 * FD)
    c["lnbase_row"] = np.log(np.concatenate([base, base]))[None, :].astype(np.float32)
    sgn = np.concatenate([-np.ones(32), np.ones(32)])
    c["signcol128"] = np.tile(sgn, 2)[:, None].astype(np.float32)
    return c


CONST_SPECS = [
    ("ident_f32", (128, 128), F32), ("ident_bf", (128, 128), BF16),
    ("negbig_ident", (128, 128), BF16),
    ("amask_diag", (128, 128), BF16), ("amask_far", (128, 128), BF16),
    ("ones3", (3, 128), BF16), ("negones3", (3, 128), BF16),
    ("ones1x128", (1, 128), F32), ("ones1x128b", (1, 128), BF16),
    ("onescol", (128, 1), F32), ("onescol_b", (128, 1), BF16),
    ("onespair", (128, 2), BF16), ("onespairT", (2, 128), BF16),
    ("iota512", (1, 512), F32), ("invf_row", (1, 64), F32), ("lnbase_row", (1, 64), F32),
    ("signcol128", (128, 1), F32),
    ("flags", (128, 8), F32), ("am_far0", (128, 128), BF16), ("am_mid0", (128, 128), BF16),
]
CONST_SRC = {"ident_bf": "ident_f32", "ones1x128b": "ones1x128", "onescol_b": "onescol"}


class Ctx:
    pass


def build_program():
    nc = bass.Bass("TRN2", target_bir_lowering=False)
    C = Ctx()
    C.nc = nc
    C.g = {}
    C.sch = Sched()
    C.stack = ExitStack()

    def dram_in(name, shape):
        C.g[name] = nc.dram_tensor(name, list(shape), F32, kind="ExternalInput").ap()

    def dram_int(name, shape, dt=F32):
        kind = "ExternalOutput" if (DEBUG_TAPS and name.startswith("scr_")) else "Internal"
        C.g[name] = nc.dram_tensor(name, list(shape), dt, kind=kind).ap()

    dram_in("x_my", (RXP, D))
    dram_in("w_inproj", (D, DIP))
    dram_in("conv_w", (CV, 4))
    dram_in("conv_b", (CV, 1))
    dram_in("dt_bias", (NH, 1))
    dram_in("a_log", (NH, 1))
    dram_in("d_vec", (NH, 1))
    dram_in("norm_w", (DI, 1))
    dram_in("w_outproj", (DI, D))
    dram_in("w_proj", (DI, D))
    dram_in("b_proj", (D, 1))
    for l in range(2):
        dram_in(f"wqkv{l}", (D, 3 * FI))
        dram_in(f"qs{l}", (FD, 1))
        dram_in(f"ks{l}", (FD, 1))
        dram_in(f"wo{l}", (FI, D))
        dram_in(f"lng{l}", (D, 1))
        dram_in(f"lnb{l}", (D, 1))
        dram_in(f"w1p{l}", (D, H2P))
        dram_in(f"b1p{l}", (H2P, 1))
        dram_in(f"w2p{l}", (HIDP, D))
        dram_in(f"b2{l}", (D, 1))
    for nm, shp, dt in CONST_SPECS:
        if nm not in CONST_SRC:
            dram_in(nm, shp)
    dram_in("maskrep_fwd", (128, 1024))
    dram_in("m1g", (64, R))
    dram_in("m2g", (64, R))
    dram_in("maskrep_bwd", (128, 1024))
    dram_in("deltapat", (96, 4096))
    C.g["y_my"] = nc.dram_tensor("y_my", [R, D], F32, kind="ExternalOutput").ap()

    dram_int("scr_xbc", (17, 128, RXP), BF16)
    dram_int("scr_z", (RXP, DI))
    dram_int("scr_dttm", (RXP, NH))
    dram_int("scr_dtfm", (NH, RXP))
    dram_int("scr_ybl", (R, DI))
    dram_int("scr_cbwd", (DST, R))
    dram_int("scr_dxt", (R, NH))
    for n in ("scr_x1", "scr_s1", "scr_s2", "scr_s3"):
        dram_int(n, (R + 1, D))
    dram_int("scr_ynf", (R, DI), BF16)
    dram_int("scr_dbg", (128, 16384))
    dram_int("scr_dbg2", (128, 16384))
    dram_int("scr_ynbw", (R, DI), BF16)
    dram_int("snd_state", (128, 2048))
    dram_int("rcv_state", (256, 2048))
    for l in range(2):
        dram_int(f"snd_k{l}", (64, 8 * 512), BF16)
        dram_int(f"rcv_k{l}", (128, 8 * 512), BF16)
        dram_int(f"snd_v{l}", (512, 512), BF16)
        dram_int(f"rcv_v{l}", (1024, 512), BF16)
        dram_int(f"snd_row{l}", (1, 1024))
        dram_int(f"rcv_row{l}", (2, 1024))
        dram_int(f"snd_mrow{l}", (1, 1024))
        dram_int(f"rcv_mrow{l}", (2, 1024))

    def S(name, shape, dt=F32):
        return C.stack.enter_context(nc.sbuf_tensor(name, list(shape), dt))

    def PS(name, shape, dt=F32):
        q = 512 if dt == F32 else 1024
        shape = list(shape)
        shape[1] = ((shape[1] + q - 1) // q) * q
        return C.stack.enter_context(nc.psum_tensor(name, list(shape), dt))

    C.S, C.P = S, PS
    C.B = lambda name="": Buf(name)

    C.cb = {}
    C.cbuf = {}
    for nm, shp, dt in CONST_SPECS:
        C.cb[nm] = S("c_" + nm, shp, dt)
        src = C.g[CONST_SRC.get(nm, nm)]
        b = C.B("c_" + nm)
        C.sch.dma("pool", C.cb[nm][:, :], src, writes=[b])
        C.cbuf[nm] = b

    C.blam = Buf("lamscr")
    C.bpbr = Buf("pbscr")
    C.fc = S("fconst", [128, 8])
    C.bfc = C.B("fconst")
    C.sch.do("pool", lambda h: h.memset(C.fc[:, 0:1], EPS), writes=[C.bfc])
    C.sch.do("pool", lambda h: h.memset(C.fc[:, 1:2], 1.0), writes=[C.bfc])

    if "ph0" in PHASES:
        ph0(C)
        C.sch.barrier()
    if "s1" in PHASES:
        scan_phase(C, phase=1)
        C.sch.barrier()
    exchange_state(C)
    C.sch.barrier()
    if "s2" in PHASES:
        scan_phase(C, phase=2)
        C.sch.barrier()
    if "proj" in PHASES:
        proj_phase(C)
        C.sch.barrier()
    for l in range(2):
        attn_phase(C, l)
        C.sch.barrier()
        mlp_phase(C, l)
        C.sch.barrier()
    final_wait(C)

    C.sch.emit(nc, C.stack)
    C.stack.close()
    return nc


def ph0(C):
    sch, S, P, B, cb = C.sch, C.S, C.P, C.B, C.cb
    g = C.g
    with ExitStack() as ph:
        def S0(n, s, dt=F32):
            return ph.enter_context(C.nc.sbuf_tensor(n, list(s), dt))

        def P0(n, s, dt=F32):
            q = 512 if dt == F32 else 1024
            s = list(s)
            s[1] = ((s[1] + q - 1) // q) * q
            return ph.enter_context(C.nc.psum_tensor(n, s, dt))

        wip = S0("p0_wip", [128, 8 * DIP], BF16)
        bwip = B("wip")
        for kc in range(8):
            sch.dma("pool", wip[:, kc * DIP:(kc + 1) * DIP],
                    g["w_inproj"][kc * 128:(kc + 1) * 128, :], writes=[bwip])
        dtb = S0("p0_dtb", [NH, 1])
        bdtb = B("dtb")
        sch.dma("pool", dtb[:, :], g["dt_bias"], writes=[bdtb])

        xt = [S0(f"p0_x{i}", [128, 1024]) for i in range(4)]
        sqt = [S0(f"p0_sq{i}", [128, 1024], BF16) for i in range(2)]
        red = S0("p0_red", [128, 8])
        ubt = [S0(f"p0_ub{i}", [128, 1024], BF16) for i in range(2)]
        ut = [S0(f"p0_u{i}", [128, 8 * 512], BF16) for i in range(2)]
        exbc = S0("p0_exbc", [128, 17 * 512], BF16)
        ez = S0("p0_ez", [128, 16 * 512])
        dtw = S0("p0_dtw", [NH, 6 * 512])
        dtt = [S0(f"p0_dtt{i}", [128, 32]) for i in range(4)]

        bxt = [B() for _ in range(4)]
        bsqt = [B() for _ in range(2)]
        bred = B("red")
        bubt = [B() for _ in range(2)]
        but = [B() for _ in range(2)]
        bexbc = B()
        bez = B()
        bdtw = B("dtw")
        bdtt = [B() for _ in range(4)]

        pf = [P0(f"p0_pf{i}", [128, 1024]) for i in range(3)]
        pslot = [(pf[i // 2], (i % 2) * 512) for i in range(6)]
        bps = [B(f"ps{i}") for i in range(6)]
        pb16 = P0("p0_pb16", [128, 1024], BF16)
        bpb16 = B("pb16")

        bscr_xbc = B("scr_xbc")
        bscr_z = B("scr_z")
        bscr_dt = B("scr_dt")

        for i in range(NPT):
            r0 = i * T
            bi = i % 2
            for tc in range(4):
                xb = xt[tc]
                bx = bxt[tc]
                rr = r0 + tc * 128
                sch.dma("pool", xb[:, :], g["x_my"][rr:rr + 128, :], writes=[bx])
                sq = sch.do("act", lambda h, xb=xb, bi=bi, tc=tc: h.activation(
                    sqt[tc % 2][:, :], xb[:, :], AF.Square), reads=[bx], writes=[bsqt[tc % 2]])
                r1 = sch.do("dve", lambda h, xb=xb, tc=tc: h.reduce_sum(
                    red[:, tc:tc + 1], xb[:, :], axis=AX), reads=[bx], writes=[bred])
                r2 = sch.do("dve", lambda h, tc=tc: h.reduce_sum(
                    red[:, 4 + tc:5 + tc], sqt[tc % 2][:, :], axis=AX), reads=[bsqt[tc % 2]], writes=[bred])
                c1 = sch.do("dve", lambda h, tc=tc: h.tensor_scalar(
                    red[:, tc:tc + 1], red[:, tc:tc + 1], 1.0 / D, None, ALU.mult),
                    reads=[bred], writes=[bred])
                c2 = sch.do("dve", lambda h, tc=tc: h.tensor_scalar(
                    red[:, 4 + tc:5 + tc], red[:, 4 + tc:5 + tc], 1.0 / D, None, ALU.mult),
                    reads=[bred], writes=[bred])
                c3 = sch.do("dve", lambda h, tc=tc: h.scalar_tensor_tensor(
                    red[:, 4 + tc:5 + tc], red[:, tc:tc + 1], red[:, tc:tc + 1],
                    red[:, 4 + tc:5 + tc], ALU.mult, ALU.subtract),
                    reads=[bred], writes=[bred])
                c4 = sch.do("act", lambda h, tc=tc: h.activation(
                    red[:, 4 + tc:5 + tc], red[:, 4 + tc:5 + tc], AF.Sqrt, bias=C.fc[:, 0:1], scale=-1.0),
                    reads=[bred, C.bfc], writes=[bred])
                c5 = sch.do("dve", lambda h, tc=tc: h.reciprocal(
                    red[:, 4 + tc:5 + tc], red[:, 4 + tc:5 + tc]), reads=[bred], writes=[bred])
                un = sch.do("dve", lambda h, xb=xb, tc=tc, bi=bi: h.tensor_scalar(
                    ubt[tc % 2][:, :], xb[:, :], red[:, tc:tc + 1], red[:, 4 + tc:5 + tc],
                    ALU.subtract, ALU.mult), reads=[bx, bred], writes=[bubt[tc % 2]])
                for fc in range(8):
                    tr = sch.do("pe", lambda h, tc=tc, fc=fc: h.transpose(
                        pb16[:, fc * 128:(fc + 1) * 128], ubt[tc % 2][:, fc * 128:(fc + 1) * 128],
                        cb["ident_bf"][:, :]),
                        reads=[bubt[tc % 2], C.cbuf["ident_bf"]], writes=[bpb16])
                ev = sch.do("act", lambda h, bi=bi, tc=tc: h.activation(
                    ut[bi].rearrange("p (c t) -> p c t", c=8)[:, :, tc * 128:tc * 128 + 128],
                    pb16.rearrange("p (c t) -> p c t", c=8), AF.Copy),
                    reads=[bpb16], writes=[but[bi]])

            for j, mc in enumerate(range(16, 34)):
                sl = j % 2
                pb, po = pslot[sl]
                mw = 32 if mc == 33 else 128
                mm = None
                for kc in range(8):
                    mm = sch.do("pe", lambda h, bi=bi, mc=mc, kc=kc, pb=pb, po=po, mw=mw, f=(kc == 0): h.matmul(
                        pb[0:mw, po:po + 512], wip[:, kc * DIP + mc * 128:kc * DIP + mc * 128 + mw],
                        ut[bi][:, kc * 512:(kc + 1) * 512], start=f, stop=(kc == 7)),
                        reads=[but[bi], bwip], writes=[bps[sl]] if kc == 0 else (), extra=[mm])
                bps[sl].writers = [mm]
                if mc < 33:
                    ev = sch.do("act", lambda h, j=j, pb=pb, po=po: h.activation(
                        exbc[:, j * 512:(j + 1) * 512], pb[:, po:po + 512], AF.Copy),
                        reads=[bps[sl]], writes=[bexbc])
                else:
                    dw = dtw
                    sch.do("dve", lambda h, dw=dw, pb=pb, po=po: h.tensor_scalar(
                        dw[:, 0:512], pb[0:NH, po:po + 512], dtb[:, 0:1], None, ALU.add),
                        reads=[bps[sl], bdtb], writes=[bdtw])
                    sch.do("act", lambda h, dw=dw: h.activation(dw[:, 512:1024], dw[:, 0:512], AF.Abs),
                           reads=[bdtw], writes=[bdtw])
                    sch.do("act", lambda h, dw=dw: h.activation(dw[:, 1024:1536], dw[:, 512:1024], AF.Exp, scale=-1.0),
                           reads=[bdtw], writes=[bdtw])
                    sch.do("act", lambda h, dw=dw: h.activation(dw[:, 1536:2048], dw[:, 1024:1536], AF.Ln, bias=C.fc[0:NH, 1:2]),
                           reads=[bdtw, C.bfc], writes=[bdtw])
                    sch.do("act", lambda h, dw=dw: h.activation(dw[:, 2048:2560], dw[:, 0:512], AF.Relu),
                           reads=[bdtw], writes=[bdtw])
                    sch.do("dve", lambda h, dw=dw: h.tensor_tensor(
                        dw[:, 2560:3072], dw[:, 1536:2048], dw[:, 2048:2560], ALU.add),
                        reads=[bdtw], writes=[bdtw])
                    sch.dma("pool", g["scr_dtfm"][:, r0:r0 + T], dw[:, 2560:3072],
                            reads=[bdtw], writes=[bscr_dt])
                    ptrp, ptro = pslot[2]
                    for tc in range(4):
                        tb = tc % 4
                        tr = sch.do("pe", lambda h, dw=dw, tc=tc, ptrp=ptrp, ptro=ptro: h.transpose(
                            ptrp[:, ptro:ptro + NH], dw[:, 2560 + tc * 128:2560 + tc * 128 + 128],
                            cb["ident_f32"][0:NH, 0:NH]),
                            reads=[bdtw, C.cbuf["ident_f32"]], writes=[bps[2]])
                        te = sch.do("act", lambda h, tb=tb, ptrp=ptrp, ptro=ptro: h.activation(
                            dtt[tb][:, 0:NH], ptrp[:, ptro:ptro + NH], AF.Copy),
                            reads=[bps[2]], writes=[bdtt[tb]])
                        sch.dma("pool", g["scr_dttm"][r0 + tc * 128:r0 + tc * 128 + 128, :], dtt[tb][:, 0:NH],
                                reads=[bdtt[tb]], writes=[bscr_dt])

            for tc in range(4):
                for zc in range(4):
                    sl = 2 + (tc * 4 + zc) % 2
                    if sl == 2 and False:
                        pass
                    sl = 3 + (tc * 4 + zc) % 2
                    pb, po = pslot[sl]
                    mm = None
                    for kc in range(8):
                        mm = sch.do("pe", lambda h, bi=bi, tc=tc, zc=zc, kc=kc, pb=pb, po=po, f=(kc == 0): h.matmul(
                            pb[:, po:po + 512], ut[bi][:, kc * 512 + tc * 128:kc * 512 + tc * 128 + 128],
                            wip[:, kc * DIP + zc * 512:kc * DIP + zc * 512 + 512],
                            start=f, stop=(kc == 7)),
                            reads=[but[bi], bwip], writes=[bps[sl]] if kc == 0 else (), extra=[mm])
                    bps[sl].writers = [mm]
                    ev = sch.do("act", lambda h, tc=tc, zc=zc, pb=pb, po=po: h.activation(
                        ez[:, (tc * 4 + zc) * 512:(tc * 4 + zc) * 512 + 512], pb[:, po:po + 512], AF.Copy),
                        reads=[bps[sl]], writes=[bez])
            sch.dma("pool", g["scr_xbc"][:, :, r0:r0 + T].transpose([1, 0, 2]),
                    exbc[:, :].rearrange("p (c q) -> p c q", c=17), reads=[bexbc], writes=[bscr_xbc])
            sch.dma("pool",
                    g["scr_z"][r0:r0 + T, :].rearrange("(tc p) (zc q) -> p tc zc q", p=128, q=512),
                    ez[:, :].rearrange("p (tc zc q) -> p tc zc q", tc=4, zc=4),
                    reads=[bez], writes=[bscr_z])
        C.bscr = {"xbc": bscr_xbc, "z": bscr_z, "dt": bscr_dt}


def scan_phase(C, phase):
    sch, S, P, B, cb = C.sch, C.S, C.P, C.B, C.cb
    g = C.g
    nc = C.nc
    with ExitStack() as ph:
        def S0(n, s, dt=F32):
            return ph.enter_context(nc.sbuf_tensor(f"p{phase}_" + n, list(s), dt))

        def P0(n, s, dt=F32):
            q = 512 if dt == F32 else 1024
            s = list(s)
            s[1] = ((s[1] + q - 1) // q) * q
            return ph.enter_context(nc.psum_tensor(f"p{phase}_" + n, s, dt))

        convw = S0("convw", [128, 17 * 4])
        bconvw = B()
        for c in range(17):
            sch.dma("pool", convw[:, c * 4:(c + 1) * 4], g["conv_w"][c * 128:(c + 1) * 128, :], writes=[bconvw])
        cvb = S0("cvb", [128, 17])
        bcvb = B()
        for c in range(17):
            sch.dma("pool", cvb[:, c:c + 1], g["conv_b"][c * 128:(c + 1) * 128, 0:1], writes=[bcvb])
        cdiag = S0("cdiag", [128, 68 * 128], BF16)
        bcdiag = B()
        for c in range(17):
            for j in range(4):
                sch.do("dve", lambda h, c=c, j=j: h.tensor_scalar(
                    cdiag[:, (c * 4 + j) * 128:(c * 4 + j) * 128 + 128], cb["ident_bf"][:, :],
                    convw[:, c * 4 + j:c * 4 + j + 1], None, ALU.mult),
                    reads=[bconvw, C.cbuf["ident_bf"]], writes=[bcdiag])
        acol = S0("acol", [NH, 2])
        bacol = B()
        sch.dma("pool", acol[:, 0:1], g["a_log"], writes=[bacol])
        sch.do("act", lambda h: h.activation(acol[:, 1:2], acol[:, 0:1], AF.Exp), reads=[bacol], writes=[bacol])
        sch.do("dve", lambda h: h.tensor_scalar(acol[:, 0:1], acol[:, 1:2], -1.0, None, ALU.mult),
               reads=[bacol], writes=[bacol])
        ones32 = S0("ones32", [NH, 128])
        bones32 = B()
        sch.do("pool", lambda h: h.memset(ones32[:, :], 1.0), writes=[bones32])
        lmask = {}
        blmask = B()
        for nm2, shp2 in (("maskrep_fwd", (128, 1024)), ("maskrep_bwd", (128, 1024)), ("deltapat", (96, 4096))):
            lmask[nm2] = S0(nm2, list(shp2), BF16)
            sch.dma("pool", lmask[nm2][:, :], g[nm2], writes=[blmask])
        dexc = S0("dexc", [128, 16])
        bdexc = B()
        for hh2 in range(2):
            sch.dma("pool", dexc[hh2 * 64:(hh2 + 1) * 64, :],
                    g["d_vec"].rearrange("(c h) one -> h (c one)", h=2)[hh2:hh2 + 1, :]
                    .broadcast_to([64, 16]), writes=[bdexc], slow=True)
        ddiag = S0("ddiag", [128, 16 * 128], BF16)
        bddiag = B()
        for c in range(16):
            sch.do("dve", lambda h, c=c: h.tensor_scalar(
                ddiag[:, c * 128:(c + 1) * 128], cb["ident_bf"][:, :], dexc[:, c:c + 1], None, ALU.mult),
                reads=[bdexc, C.cbuf["ident_bf"]], writes=[bddiag])

        dirs = ["bwd", "fwd"]
        full = {"bwd": phase == 1, "fwd": phase == 2}

        pf = [P0(f"pf{i}", [128, 1024]) for i in range(3)]
        bpf = [B(f"pf{i}") for i in range(3)]
        pb16 = P0("pb16", [128, 1024], BF16)
        bpb16 = B("pb16")
        psm = P0("psm", [128, 512])
        bpsm = B("psm")

        xin = S0("xin", [128, 17 * 520], BF16)
        bxin = B()
        xbca = {}
        bxbca = {}
        for d in dirs:
            if phase == 2 and d == "bwd":
                continue
            xbca[d] = S0(f"xbca_{d}", [128, 17 * 512], BF16)
            bxbca[d] = B()
        cfm = {d: S0(f"cfm_{d}", [64, 512], BF16) for d in xbca}
        bcfm = {d: B() for d in xbca}
        dtfmt = S0("dtfmt", [NH, 512])
        bdtfmt = B()
        dttmt = S0("dttmt", [128, 32])
        bdttmt = B()
        afm = S0("afm", [NH, 128])
        bafm = B()
        pac = S0("pac", [NH, 129 * 2])
        bpac = B()
        spl = S0("spl", [NH, 129 * 8], BF16)
        bspl = B()
        lhsA = S0("lhsA", [96, 128], BF16)
        blhsA = B()
        flat = S0("flat", [3, 4096], BF16)
        bflat = B()
        lall = S0("lall", [128, 4096], BF16)
        blall = B()
        mstb = S0("mstb", [128, 128], BF16)
        bmstb = B()
        btm = S0("btm", [128, 64], BF16)
        bbtm = B()
        xhtm = S0("xhtm", [128, 2048], BF16)
        bxhtm = B()
        dtx = xhtm
        bdtx = bxhtm
        dtxw = S0("dtxw", [128, 2048], BF16)
        bdtxw = B()
        dtb16 = S0("dtb16", [128, 32], BF16)
        bdtb16 = B()
        ptm = S0("ptm", [128, 32 * 4])
        bptm = B()
        erow = S0("erow", [1, 32 * 4])
        berow = B()
        wde = S0("wde", [128, 32 * 2], BF16)
        bwde = B()
        dexf = S0("dexf", [128, 32])
        bdexf = B()
        lamS = S0("lamS", [64, 64])
        blamS = B()
        ssb = [S0(f"ssb{i}", [64, 2048]) for i in range(2)]
        bssb = [B() for _ in range(2)]
        sbf2 = [S0(f"sbf{i2}", [64, 1024], BF16) for i2 in range(2)]
        bsbf = B()
        yoff = S0("yoff", [128, 2048])
        byoff = B()
        ytm = S0("ytm", [128, 2048])
        bytm = B()

        for i2 in range(2):
            sch.do("pool", lambda h, i2=i2: h.memset(ssb[i2][:, :], 0.0), writes=[bssb[i2]])
        if phase == 1:
            sacc = [S0(f"sacc{i2}", [64, 2048]) for i2 in range(2)]
            bsacc = [B() for _ in range(2)]
            dxtt = S0("dxtt", [128, 32])
            bdxtt = B()
            for i2 in range(2):
                sch.do("pool", lambda h, i2=i2: h.memset(sacc[i2][:, :], 0.0), writes=[bsacc[i2]])
            sch.do("pool", lambda h: h.memset(erow[0:1, 64:96], 1.0), writes=[berow])
            for i2 in range(2):
                sch.do("pool", lambda h, i2=i2: h.memset(sbf2[i2][:, :], 0.0), writes=[bsbf])
        else:
            zt = S0("zt", [128, 2048])
            bzt = B()
            gt = S0("gt", [128, 2048], BF16)
            bgt = B()
            ybl = S0("ybl", [128, 2048])
            bybl = B()
            cbw = S0("cbw", [64, 128], BF16)
            bcbw = B()
            dxtl = S0("dxtl", [128, 32])
            bdxtl = B()
            ygf = lall[:, :].bitcast(F32)
            bygf = blall
            sqb = S0("sqb", [128, 2048], BF16)
            bsqb = B()
            rms = S0("rms", [128, 4])
            brms = B()
            ynb = sqb
            bynb = bsqb
            sinitb2 = [S0(f"sinitb{i2}", [64, 1024], BF16) for i2 in range(2)]
            bsinitb = B()
            sch.dma("pool", zt[0:64, :], g["rcv_state"][0:64, :], reads=[C.brcv], writes=[bzt])
            sch.do("dve", lambda h: h.tensor_scalar(ssb[0][:, :], zt[0:64, :], cb["flags"][0:64, 0:1], None, ALU.mult),
                   reads=[bzt, C.cbuf["flags"]], writes=[bssb[0]])
            for i2 in range(2):
                sch.do("dve", lambda h, i2=i2: h.tensor_copy(sbf2[i2][:, :], ssb[0][:, i2 * 1024:(i2 + 1) * 1024]),
                       reads=[bssb[0]], writes=[bsbf])
            sch.dma("pool", zt[0:64, :], g["rcv_state"][192:256, :], reads=[C.brcv], writes=[bzt])
            for i2 in range(2):
                sch.do("dve", lambda h, i2=i2: h.tensor_scalar(
                    sinitb2[i2][:, :], zt[0:64, i2 * 1024:(i2 + 1) * 1024], cb["flags"][0:64, 1:2], None, ALU.mult),
                    reads=[bzt, C.cbuf["flags"]], writes=[bsinitb])

        bscr_ybl = B("w_ybl")
        bscr_cb = B("w_cb")
        bscr_dxt = B("w_dxt")
        bscr_x1 = B("w_x1")

        tiles = list(range(NT - 1, -1, -1)) if phase == 1 else list(range(NT))
        sidx = 0

        for ti, i in enumerate(tiles):
            r0 = i * T
            sch.dma("pool", xin[:, :].rearrange("p (c q) -> p c q", c=17)[:, :, 0:518],
                    g["scr_xbc"][:, :, r0:r0 + 518].transpose([1, 0, 2]), writes=[bxin])
            for d in (["bwd", "fwd"] if phase == 1 else ["fwd"]):
                off = 3 if d == "bwd" else 0
                for c in range(17):
                    mm = None
                    sl = c % 6
                    pslot, pso = pf[sl // 2], (sl % 2) * 512
                    pbuf = bpf[sl // 2]
                    for j in range(4):
                        wj = j if d == "fwd" else 3 - j
                        mm = sch.do("pe", lambda h, c=c, j=j, wj=wj, off=off, pslot=pslot, pso=pso, f=(j == 0): h.matmul(
                            pslot[:, pso:pso + 512], cdiag[:, (c * 4 + wj) * 128:(c * 4 + wj) * 128 + 128],
                            xin[:, c * 520 + off + j:c * 520 + off + j + 512], start=f, stop=(j == 3)),
                            reads=[bxin, bcdiag], writes=[pbuf] if j == 0 else (), extra=[mm])
                    pbuf.writers = [mm]
                    sch.do("act", lambda h, d=d, c=c, pslot=pslot, pso=pso: h.activation(
                        xbca[d][:, c * 512:(c + 1) * 512], pslot[:, pso:pso + 512], AF.Silu, bias=cvb[:, c:c + 1]),
                        reads=[pbuf, bcvb], writes=[bxbca[d]])
                    if c == 16:
                        sch.do("act", lambda h, d=d, pslot=pslot, pso=pso: h.activation(
                            cfm[d][:, :], pslot[64:128, pso:pso + 512], AF.Silu, bias=cvb[64:128, 16:17]),
                            reads=[pbuf, bcvb], writes=[bcfm[d]])
                if phase == 1 and d == "bwd":
                    sch.dma("pool", g["scr_cbwd"][:, r0:r0 + 512], cfm[d][:, :], reads=[bcfm[d]], writes=[bscr_cb])
            sch.dma("pool", dtfmt[:, :], g["scr_dtfm"][:, r0 + 3:r0 + 3 + 512], writes=[bdtfmt])

            chunks = list(range(3, -1, -1)) if phase == 1 else list(range(4))
            for k in chunks:
                tok0 = i * 512 + k * 128
                sch.dma("pool", dttmt[:, :], g["scr_dttm"][r0 + 3 + k * 128:r0 + 3 + k * 128 + 128, :],
                        writes=[bdttmt])
                if phase == 2:
                    sch.dma("pool", zt[:, :], g["scr_z"][r0 + 3 + k * 128:r0 + 3 + k * 128 + 128, :], writes=[bzt])
                    sch.dma("pool", ybl[:, :], g["scr_ybl"][tok0:tok0 + 128, :], writes=[bybl])
                    sch.dma("pool", cbw[:, :], g["scr_cbwd"][:, tok0:tok0 + 128], writes=[bcbw])
                    sch.dma("pool", dxtl[:, :], g["scr_dxt"][tok0:tok0 + 128, :], writes=[bdxtl])

                sch.do("dve", lambda h, k=k: h.tensor_scalar(
                    afm[:, :], dtfmt[:, k * 128:(k + 1) * 128], acol[:, 0:1], None, ALU.mult),
                    reads=[bdtfmt, bacol], writes=[bafm])
                sch.do("pool", lambda h: h.memset(pac[:, 0:1], 0.0), writes=[bpac])
                sch.do("dve", lambda h: h.tensor_tensor_scan(
                    pac[:, 1:129], ones32[:, :], afm[:, :], 0.0, ALU.mult, ALU.add),
                    reads=[bones32, bafm], writes=[bpac])
                sch.do("dve", lambda h: h.tensor_copy(spl[:, 0:129], pac[:, 0:129]), reads=[bpac], writes=[bspl])
                sch.do("dve", lambda h: h.tensor_tensor(pac[:, 129:258], pac[:, 0:129], spl[:, 0:129], ALU.subtract),
                       reads=[bpac, bspl], writes=[bpac])
                sch.do("dve", lambda h: h.tensor_copy(spl[:, 129:258], pac[:, 129:258]), reads=[bpac], writes=[bspl])
                sch.do("dve", lambda h: h.tensor_tensor(pac[:, 129:258], pac[:, 129:258], spl[:, 129:258], ALU.subtract),
                       reads=[bpac, bspl], writes=[bpac])
                sch.do("dve", lambda h: h.tensor_copy(spl[:, 258:387], pac[:, 129:258]), reads=[bpac], writes=[bspl])
                sch.do("dve", lambda h: h.tensor_scalar(spl[:, 387:774], spl[:, 0:387], -1.0, None, ALU.mult),
                       reads=[bspl], writes=[bspl])
                sch.do("pe", lambda h: h.transpose(psm[:, 0:32], pac[:, 1:129], cb["ident_f32"][0:NH, 0:NH]),
                       reads=[bpac, C.cbuf["ident_f32"]], writes=[bpsm])
                sch.do("act", lambda h: h.activation(ptm[:, 0:32], psm[:, 0:32], AF.Copy), reads=[bpsm], writes=[bptm])
                sch.do("pe", lambda h: h.transpose(psm[:, 64:96], pac[:, 0:128], cb["ident_f32"][0:NH, 0:NH]),
                       reads=[bpac, C.cbuf["ident_f32"]], writes=[bpsm])
                sch.do("act", lambda h: h.activation(ptm[:, 32:64], psm[:, 64:96], AF.Copy), reads=[bpsm], writes=[bptm])
                sch.do("pe", lambda h: h.transpose(psm[0:1, 128:160], pac[:, 128:129], cb["ident_f32"][0:NH, 0:NH]),
                       reads=[bpac, C.cbuf["ident_f32"]], writes=[bpsm])
                sch.do("act", lambda h: h.activation(erow[0:1, 0:32], psm[0:1, 128:160], AF.Copy),
                       reads=[bpsm], writes=[berow])
                sch.do("act", lambda h: h.activation(erow[0:1, 32:64], erow[0:1, 0:32], AF.Exp),
                       reads=[berow], writes=[berow])
                sch.do("pe", lambda h: h.matmul(psm[:, 448:480], cb["ones1x128"][:, :], erow[0:1, 0:32],
                                                start=True, stop=True),
                       reads=[berow, C.cbuf["ones1x128"]], writes=[bpsm])

                for d in dirs:
                    if phase == 2 and d == "bwd":
                        continue
                    isf = (d == "fwd")
                    if isf:
                        sch.do("dve", lambda h: h.tensor_tensor(ptm[:, 96:128], psm[:, 448:480], ptm[:, 0:32], ALU.subtract),
                               reads=[bpsm, bptm], writes=[bptm])
                        sch.do("act", lambda h: h.activation(wde[:, 0:32], ptm[:, 96:128], AF.Exp),
                               reads=[bptm], writes=[bwde])
                        if full[d]:
                            sch.do("act", lambda h: h.activation(dexf[:, :], ptm[:, 0:32], AF.Exp),
                                   reads=[bptm], writes=[bdexf])
                    else:
                        sch.do("act", lambda h: h.activation(wde[:, 32:64], ptm[:, 32:64], AF.Exp),
                               reads=[bptm], writes=[bwde])
                        sch.do("dve", lambda h: h.tensor_tensor(ptm[:, 96:128], psm[:, 448:480], ptm[:, 32:64], ALU.subtract),
                               reads=[bpsm, bptm], writes=[bptm])
                        sch.do("act", lambda h: h.activation(dexf[:, :], ptm[:, 96:128], AF.Exp),
                               reads=[bptm], writes=[bdexf])

                    for half in range(2):
                        for cc in range(8):
                            c = half * 8 + cc
                            sch.do("pe", lambda h, d=d, c=c, cc=cc, k=k: h.transpose(
                                pb16[:, cc * 128:(cc + 1) * 128],
                                xbca[d][:, c * 512 + k * 128:c * 512 + k * 128 + 128], cb["ident_bf"][:, :]),
                                reads=[bxbca[d], C.cbuf["ident_bf"]], writes=[bpb16])
                        sch.do("act", lambda h, half=half: h.activation(
                            xhtm[:, half * 1024:(half + 1) * 1024], pb16[:, :], AF.Copy),
                            reads=[bpb16], writes=[bxhtm])
                    sch.do("dve", lambda h: h.tensor_copy(dtb16[:, :], dttmt[:, :]), reads=[bdttmt], writes=[bdtb16])
                    sch.do("dve", lambda h: h.tensor_tensor(
                        dtx[:, :].rearrange("p (hh j) -> p hh j", hh=32), xhtm[:, :].rearrange("p (hh j) -> p hh j", hh=32),
                        dtb16[:, :].unsqueeze(2).broadcast_to([128, 32, 64]), ALU.mult),
                        reads=[bxhtm, bdtb16], writes=[bdtx])
                    ws0 = 0 if isf else 32
                    sch.do("dve", lambda h, ws0=ws0: h.tensor_tensor(
                        dtxw[:, :].rearrange("p (hh j) -> p hh j", hh=32), dtx[:, :].rearrange("p (hh j) -> p hh j", hh=32),
                        wde[:, ws0:ws0 + 32].unsqueeze(2).broadcast_to([128, 32, 64]), ALU.mult),
                        reads=[bdtx, bwde], writes=[bdtxw])
                    sch.do("pe", lambda h, d=d, k=k: h.transpose(
                        pb16[:, 0:64], xbca[d][0:64, 16 * 512 + k * 128:16 * 512 + k * 128 + 128],
                        cb["ident_bf"][0:64, 0:64]),
                        reads=[bxbca[d], C.cbuf["ident_bf"]], writes=[bpb16])
                    sch.do("act", lambda h: h.activation(btm[:, :], pb16[:, 0:64], AF.Copy),
                           reads=[bpb16], writes=[bbtm])

                    if full[d]:
                        sch.do("pe", lambda h, d=d, k=k: h.matmul(
                            psm[:, 128:256], xbca[d][0:64, 16 * 512 + k * 128:16 * 512 + k * 128 + 128],
                            cfm[d][:, k * 128:(k + 1) * 128], start=True, stop=True),
                            reads=[bxbca[d], bcfm[d]], writes=[bpsm])
                        sch.do("act", lambda h: h.activation(mstb[:, :], psm[:, 128:256], AF.Copy),
                               reads=[bpsm], writes=[bmstb])
                        flsl = (1, 129) if isf else (0, 128)
                        for sp in range(3):
                            sch.dma("pool", flat[sp:sp + 1, :].rearrange("a (p q) -> a p q", p=32),
                                    spl[:, sp * 129 + flsl[0]:sp * 129 + flsl[1]].unsqueeze(1),
                                    reads=[bspl], writes=[bflat])
                        for sp in range(3):
                            src0 = (387 if isf else 0) + sp * 129 + (1 if isf else 0)
                            sch.do("dve", lambda h, sp=sp, src0=src0: h.tensor_copy(
                                lhsA[sp * 32:(sp + 1) * 32, :], spl[:, src0:src0 + 128]),
                                reads=[bspl], writes=[blhsA])
                        for gq in range(4):
                            pg, bpg = pf[gq % 2], bpf[gq % 2]
                            mrep = "maskrep_fwd" if isf else "maskrep_bwd"
                            o3 = "ones3" if isf else "negones3"
                            mm = None
                            for nq in range(2):
                                mm = sch.do("pe", lambda h, nq=nq, pg=pg, mrep=mrep: h.matmul(
                                    pg[:, nq * 512:(nq + 1) * 512], cb["negbig_ident"][:, :],
                                    lmask[mrep][:, nq * 512:(nq + 1) * 512],
                                    start=True, stop=False),
                                    reads=[C.cbuf["negbig_ident"], blmask],
                                    writes=[bpg] if nq == 0 else (), extra=[mm])
                            for nq in range(2):
                                mm = sch.do("pe", lambda h, gq=gq, nq=nq, pg=pg: h.matmul(
                                    pg[:, nq * 512:(nq + 1) * 512], lhsA[:, :],
                                    lmask["deltapat"][:, gq * 1024 + nq * 512:gq * 1024 + (nq + 1) * 512],
                                    start=False, stop=False),
                                    reads=[blhsA, blmask], extra=[mm])
                            for nq in range(2):
                                mm = sch.do("pe", lambda h, gq=gq, nq=nq, pg=pg, o3=o3: h.matmul(
                                    pg[:, nq * 512:(nq + 1) * 512], cb[o3][:, :],
                                    flat[:, gq * 1024 + nq * 512:gq * 1024 + (nq + 1) * 512],
                                    start=False, stop=True),
                                    reads=[bflat, C.cbuf[o3]], extra=[mm])
                            bpg.writers = [mm]
                            if DEBUG_TAPS and False and phase == 2 and ti == 0 and k == 0 and gq == 0:
                                xdbgb = B("xdbg")
                                sch.do("act", lambda h, pg=pg: h.activation(
                                    ytm[:, 0:1024], pg[:, :], AF.Copy), reads=[bpg], writes=[bytm])
                                sch.dma("pool", g["scr_dbg2"][:, 12320:13344], ytm[:, 0:1024],
                                        reads=[bytm], writes=[xdbgb])
                            sch.do("act", lambda h, gq=gq, pg=pg: h.activation(
                                lall[:, gq * 1024:(gq + 1) * 1024], pg[:, :], AF.Exp),
                                reads=[bpg], writes=[blall])
                        sch.do("dve", lambda h: h.tensor_tensor(
                            lall[:, :].rearrange("p (hh t) -> p hh t", hh=32),
                            lall[:, :].rearrange("p (hh t) -> p hh t", hh=32),
                            mstb[:, :].unsqueeze(1).broadcast_to([128, 32, 128]), ALU.mult),
                            reads=[blall, bmstb], writes=[blall])

                    dsmm = None
                    for hf in range(2):
                        for nq in range(2):
                            first = (hf == 0 and nq == 0)
                            dsmm = sch.do("pe", lambda h, hf=hf, nq=nq: h.matmul(
                                pf[2][hf * 64:hf * 64 + 64, nq * 512:(nq + 1) * 512], btm[:, :],
                                dtxw[:, hf * 1024 + nq * 512:hf * 1024 + (nq + 1) * 512],
                                start=True, stop=True),
                                reads=[bbtm, bdtxw], writes=[bpf[2]] if first else (), extra=[dsmm])
                    bpf[2].writers = [dsmm]

                    if full[d]:
                        scur, bscur = ssb[sidx], bssb[sidx]
                        if phase == 2 and DEBUG_TAPS and False and ti == 0 and k == 0:
                            dbg2 = B("dbg2")
                            sch.dma("pool", g["scr_dbg2"][0:64, 0:2048], scur[:, :], reads=[bscur], writes=[dbg2])
                            sch.dma("pool", g["scr_dbg2"][:, 2048:6144], lall[:, :], reads=[blall], writes=[dbg2])
                            sch.dma("pool", g["scr_dbg2"][:, 6144:6176], dexf[:, :], reads=[bdexf], writes=[dbg2])
                            sch.dma("pool", g["scr_dbg2"][0:64, 6176 - 2048 + 2048:6176] if False else g["scr_dbg2"][0:64, 2048:2048], g["scr_dbg2"][0:1, 0:0], reads=[], writes=[dbg2]) if False else None
                            sch.dma("pool", g["scr_dbg2"][:, 8224 - 2048:8224], dtx[:, :], reads=[bdtx], writes=[dbg2])
                        for hf in range(2):
                            pg, bpg = pf[hf], bpf[hf]
                            mm = None
                            for nq in range(2):
                                mm = sch.do("pe", lambda h, hf=hf, nq=nq, pg=pg, d=d, k=k: h.matmul(
                                    pg[:, nq * 512:(nq + 1) * 512], cfm[d][:, k * 128:(k + 1) * 128],
                                    sbf2[hf][:, nq * 512:(nq + 1) * 512],
                                    start=True, stop=True),
                                    reads=[bcfm[d], bsbf], writes=[bpg] if nq == 0 else (), extra=[mm])
                            bpg.writers = [mm]
                            sch.do("dve", lambda h, hf=hf, pg=pg: h.tensor_tensor(
                                yoff[:, hf * 1024:(hf + 1) * 1024].rearrange("p (hh j) -> p hh j", hh=16),
                                pg[:, :].rearrange("p (hh j) -> p hh j", hh=16),
                                dexf[:, hf * 16:(hf + 1) * 16].unsqueeze(2).broadcast_to([128, 16, 64]), ALU.mult),
                                reads=[bpg, bdexf], writes=[byoff])
                        for hf in range(2):
                            pg, bpg = pf[hf], bpf[hf]
                            mm = None
                            for cc in range(8):
                                c = hf * 8 + cc
                                mm = sch.do("pe", lambda h, d=d, c=c, cc=cc, k=k, pg=pg: h.matmul(
                                    pg[:, cc * 128:(cc + 1) * 128],
                                    xbca[d][:, c * 512 + k * 128:c * 512 + k * 128 + 128],
                                    ddiag[:, c * 128:(c + 1) * 128],
                                    start=(cc % 4 == 0), stop=False),
                                    reads=[bxbca[d], bddiag], writes=[bpg] if cc == 0 else (), extra=[mm])
                            for hh in range(16):
                                hgl = hf * 16 + hh
                                mm = sch.do("pe", lambda h, hgl=hgl, hh=hh, pg=pg: h.matmul(
                                    pg[:, hh * 64:(hh + 1) * 64],
                                    lall[:, hgl * 128:(hgl + 1) * 128],
                                    dtx[:, hgl * 64:(hgl + 1) * 64],
                                    start=False, stop=True),
                                    reads=[blall, bdtx], extra=[mm])
                            bpg.writers = [mm]
                            sch.do("dve", lambda h, hf=hf, pg=pg: h.tensor_tensor(
                                ytm[:, hf * 1024:(hf + 1) * 1024], pg[:, :], yoff[:, hf * 1024:(hf + 1) * 1024], ALU.add),
                                reads=[bpg, byoff], writes=[bytm])
                        sch.do("pe", lambda h: h.matmul(psm[0:64, 288:320], cb["ones1x128"][0:1, 0:64],
                                                        erow[0:1, 32:64], start=True, stop=True),
                               reads=[berow, C.cbuf["ones1x128"]], writes=[bpsm])
                        sch.do("act", lambda h: h.activation(lamS[:, 0:32], psm[0:64, 288:320], AF.Copy),
                               reads=[bpsm], writes=[blamS])
                        snxt, bsnxt = ssb[1 - sidx], bssb[1 - sidx]
                        for hf2 in range(2):
                            sch.do("dve", lambda h, snxt=snxt, scur=scur, hf2=hf2: h.tensor_tensor(
                                snxt[:, hf2 * 1024:(hf2 + 1) * 1024].rearrange("n (kk p) -> n kk p", kk=16),
                                scur[:, hf2 * 1024:(hf2 + 1) * 1024].rearrange("n (kk p) -> n kk p", kk=16),
                                lamS[:, hf2 * 16:hf2 * 16 + 16].unsqueeze(2).broadcast_to([64, 16, 64]), ALU.mult),
                                reads=[bscur, blamS], writes=[bsnxt])
                            sch.do("dve", lambda h, snxt=snxt, hf2=hf2: h.tensor_tensor(
                                snxt[:, hf2 * 1024:(hf2 + 1) * 1024], snxt[:, hf2 * 1024:(hf2 + 1) * 1024],
                                pf[2][hf2 * 64:hf2 * 64 + 64, 0:1024], ALU.add),
                                reads=[bsnxt, bpf[2]], writes=[bsnxt])
                        for i2 in range(2):
                            sch.do("dve", lambda h, snxt=snxt, i2=i2: h.tensor_copy(
                                sbf2[i2][:, :], snxt[:, i2 * 1024:(i2 + 1) * 1024]),
                                reads=[bsnxt], writes=[bsbf])
                        sidx = 1 - sidx
                    else:
                        sch.do("pe", lambda h: h.matmul(psm[0:64, 288:320], cb["ones1x128"][0:1, 0:64],
                                                        erow[0:1, 64:96], start=True, stop=True),
                               reads=[berow, C.cbuf["ones1x128"]], writes=[bpsm])
                        sch.do("act", lambda h: h.activation(lamS[:, 32:64], psm[0:64, 288:320], AF.Copy),
                               reads=[bpsm], writes=[blamS])
                        for hf2 in range(2):
                            sch.do("dve", lambda h, hf2=hf2: h.tensor_tensor(
                                yoff[0:64, hf2 * 1024:(hf2 + 1) * 1024].rearrange("n (kk p) -> n kk p", kk=16),
                                pf[2][hf2 * 64:hf2 * 64 + 64, 0:1024].rearrange("n (kk p) -> n kk p", kk=16),
                                lamS[:, 32 + hf2 * 16:32 + hf2 * 16 + 16].unsqueeze(2).broadcast_to([64, 16, 64]),
                                ALU.mult),
                                reads=[bpf[2], blamS], writes=[byoff])
                        sch.do("dve", lambda h: h.tensor_tensor(sacc[1][:, :], sacc[0][:, :], yoff[0:64, :], ALU.add),
                               reads=[bsacc[0], byoff], writes=[bsacc[1]])
                        sch.do("dve", lambda h: h.tensor_copy(sacc[0][:, :], sacc[1][:, :]),
                               reads=[bsacc[1]], writes=[bsacc[0]])

                    if phase == 1 and d == "bwd" and DEBUG_TAPS and False and ti == 0 and k == 3:
                        dbg = B("dbg")
                        sch.dma("pool", g["scr_dbg"][:, 0:4096], lall[:, :], reads=[blall], writes=[dbg])
                        sch.dma("pool", g["scr_dbg"][:, 4096:6144], xhtm[:, :], reads=[bxhtm], writes=[dbg])
                        sch.dma("pool", g["scr_dbg"][:, 6144:8192], dtxw[:, :], reads=[bdtxw], writes=[dbg])
                        sch.dma("pool", g["scr_dbg"][0:NH, 8192:8450], pac[:, 0:258], reads=[bpac], writes=[dbg])
                        sch.dma("pool", g["scr_dbg"][:, 8450:10498], ytm[:, :], reads=[bytm], writes=[dbg])
                        sch.dma("pool", g["scr_dbg"][:, 10498:10626], mstb[:, :], reads=[bmstb], writes=[dbg])
                        sch.dma("pool", g["scr_dbg"][:, 10626:10690], btm[:, :], reads=[bbtm], writes=[dbg])
                        sch.dma("pool", g["scr_dbg"][:, 10690:12738], yoff[:, :], reads=[byoff], writes=[dbg])
                        sch.dma("pool", g["scr_dbg"][:, 12738:12770], dexf[:, :], reads=[bdexf], writes=[dbg])
                        sch.dma("pool", g["scr_dbg"][:, 12770:12802], dtb16[:, :], reads=[bdtb16], writes=[dbg])
                    if phase == 1 and d == "bwd":
                        sch.do("pe", lambda h: h.matmul(psm[:, 256:288], cb["ones1x128"][:, :], erow[0:1, 64:96],
                                                        start=True, stop=True),
                               reads=[berow, C.cbuf["ones1x128"]], writes=[bpsm])
                        sch.do("dve", lambda h: h.tensor_tensor(dxtt[:, :], dexf[:, :], psm[:, 256:288], ALU.mult),
                               reads=[bdexf, bpsm], writes=[bdxtt])
                        sch.dma("pool", g["scr_dxt"][tok0:tok0 + 128, :], dxtt[:, :], reads=[bdxtt], writes=[bscr_dxt])
                        sch.dma("pool", g["scr_ybl"][tok0:tok0 + 128, :], ytm[:, :], reads=[bytm], writes=[bscr_ybl])

                    if phase == 2 and d == "fwd":
                        for hf in range(2):
                            pg, bpg = pf[hf], bpf[hf]
                            mm = None
                            for nq in range(2):
                                mm = sch.do("pe", lambda h, hf=hf, nq=nq, pg=pg: h.matmul(
                                    pg[:, nq * 512:(nq + 1) * 512], cbw[:, :],
                                    sinitb2[hf][:, nq * 512:(nq + 1) * 512],
                                    start=True, stop=True),
                                    reads=[bcbw, bsinitb], writes=[bpg] if nq == 0 else (), extra=[mm])
                            bpg.writers = [mm]
                            sch.do("dve", lambda h, hf=hf, pg=pg: h.tensor_tensor(
                                yoff[:, hf * 1024:(hf + 1) * 1024].rearrange("p (hh j) -> p hh j", hh=16),
                                pg[:, :].rearrange("p (hh j) -> p hh j", hh=16),
                                dxtl[:, hf * 16:(hf + 1) * 16].unsqueeze(2).broadcast_to([128, 16, 64]), ALU.mult),
                                reads=[bpg, bdxtl], writes=[byoff])
                        sch.do("dve", lambda h: h.tensor_tensor(yoff[:, :], yoff[:, :], ybl[:, :], ALU.add),
                               reads=[byoff, bybl], writes=[byoff])
                        if DEBUG_TAPS and False and ti == 1 and k == 1:
                            dbg2b = B("dbg2b")
                            sch.dma("pool", g["scr_dbg2"][:, 6176:8224], ytm[:, :], reads=[bytm], writes=[dbg2b])
                            sch.dma("pool", g["scr_dbg2"][:, 8224:10272], yoff[:, :], reads=[byoff], writes=[dbg2b])
                        if DEBUG_TAPS and False and ti == 0 and k == 0:
                            dbg2c = B("dbg2c")
                            sch.dma("pool", g["scr_dbg2"][:, 10272:12320], ytm[:, :], reads=[bytm], writes=[dbg2c])
                            sch.dma("pool", g["scr_dbg2"][:, 12320:14368], yoff[:, :], reads=[byoff], writes=[dbg2c])
                            sch.dma("pool", g["scr_dbg2"][:, 14368:16384], zt[:, 0:2016], reads=[bzt], writes=[dbg2c])
                        sch.do("act", lambda h: h.activation(gt[:, :], zt[:, :], AF.Silu), reads=[bzt], writes=[bgt])
                        for dd, ysrc, bysrc in (("fwd", ytm, bytm), ("bwd", yoff, byoff)):
                            sch.do("dve", lambda h, ysrc=ysrc: h.tensor_tensor(ygf[:, :], ysrc[:, :], gt[:, :], ALU.mult),
                                   reads=[bysrc, bgt], writes=[bygf], strict=True)
                            sch.do("act", lambda h: h.activation(sqb[:, :], ygf[:, :], AF.Square),
                                   reads=[bygf], writes=[bsqb], strict=True)
                            sch.do("dve", lambda h: h.reduce_sum(rms[:, 0:1], sqb[:, :], axis=AX),
                                   reads=[bsqb], writes=[brms], strict=True)
                            sch.do("act", lambda h: h.activation(rms[:, 1:2], rms[:, 0:1], AF.Sqrt,
                                                                 bias=C.fc[:, 0:1], scale=1.0 / DI),
                                   reads=[brms, C.bfc], writes=[brms], strict=True)
                            sch.do("dve", lambda h: h.reciprocal(rms[:, 2:3], rms[:, 1:2]), reads=[brms], writes=[brms], strict=True)
                            sch.do("dve", lambda h: h.tensor_scalar(ynb[:, :], ygf[:, :], rms[:, 2:3], None, ALU.mult),
                                   reads=[bygf, brms], writes=[bynb], strict=True)
                            scr_yn = g["scr_ynf"] if dd == "fwd" else g["scr_ynbw"]
                            sch.dma("pool", scr_yn[tok0:tok0 + 128, :], ynb[:, :],
                                    reads=[bynb], writes=[bscr_x1])

                if phase == 1:
                    sch.do("dve", lambda h: h.tensor_tensor(erow[0:1, 96:128], erow[0:1, 64:96], erow[0:1, 32:64], ALU.mult),
                           reads=[berow], writes=[berow])
                    sch.do("dve", lambda h: h.tensor_copy(erow[0:1, 64:96], erow[0:1, 96:128]),
                           reads=[berow], writes=[berow])

        if phase == 1:
            bsnd = B("snd_state")
            sch.dma("pool", g["snd_state"][0:64, :], sacc[0][:, :], reads=[bsacc[0]], writes=[bsnd])
            sch.dma("pool", g["snd_state"][64:128, :], ssb[sidx][:, :], reads=[bssb[sidx]], writes=[bsnd])
            C.bsnd = bsnd
        else:
            C.byn_w = bscr_x1


def exchange_state(C):
    sch, g = C.sch, C.g
    if not hasattr(C, "bsnd"):
        C.bsnd = C.B("snd0")
        sch.dma("pool", g["snd_state"][0:64, :], C.fc[:, 0:8].broadcast_to([64, 2048]) if False else g["rcv_state"][0:64, :], writes=[C.bsnd])
    brcv = C.B("rcv_state")
    if NO_CC:
        sch.dma("pool", g["rcv_state"][0:128, :], g["snd_state"][:, :], reads=[C.bsnd], writes=[brcv])
        sch.dma("pool", g["rcv_state"][128:256, :], g["snd_state"][:, :], reads=[C.bsnd], writes=[brcv])
    else:
        sch.do("pool", lambda h: h.collective_compute(
            "AllGather", ALU.bypass, replica_groups=[[0, 1], [2, 3], [4, 5], [6, 7]],
            ins=[g["snd_state"].opt()], outs=[g["rcv_state"].opt()]),
            reads=[C.bsnd], writes=[brcv], is_cc=True)
    C.brcv = brcv


def proj_phase(C):
    sch, B, cb = C.sch, C.B, C.cb
    g = C.g
    nc = C.nc
    with ExitStack() as ph:
        def S0(n, s, dt=F32):
            return ph.enter_context(nc.sbuf_tensor("pp_" + n, list(s), dt))

        def P0(n, s, dt=F32):
            q = 512 if dt == F32 else 1024
            s = list(s)
            s[1] = ((s[1] + q - 1) // q) * q
            return ph.enter_context(nc.psum_tensor("pp_" + n, s, dt))

        nwc = S0("nwc", [128, 16])
        bnwc = B()
        for c in range(16):
            sch.dma("pool", nwc[:, c:c + 1], g["norm_w"][c * 128:(c + 1) * 128, 0:1], writes=[bnwc])
        wof = S0("wof", [128, 16 * 1024], BF16)
        bwof = B()
        for kc in range(16):
            sch.dma("pool", wof[:, kc * 1024:(kc + 1) * 1024],
                    g["w_outproj"][kc * 128:(kc + 1) * 128, :], writes=[bwof])
        for kc in range(16):
            sch.do("dve", lambda h, kc=kc: h.tensor_scalar(
                wof[:, kc * 1024:(kc + 1) * 1024], wof[:, kc * 1024:(kc + 1) * 1024],
                nwc[:, kc:kc + 1], None, ALU.mult), reads=[bnwc, bwof], writes=[bwof])
        wpj = S0("wpj", [128, 16 * 1024], BF16)
        bwpj = B()
        for kc in range(16):
            sch.dma("pool", wpj[:, kc * 1024:(kc + 1) * 1024],
                    g["w_proj"][kc * 128:(kc + 1) * 128, :], writes=[bwpj])
        bprow_ = S0("bprow", [1, 1024], BF16)
        bbprow = B()
        sch.dma("pool", bprow_[0:1, :], g["b_proj"].rearrange("d one -> one d"), writes=[bbprow])

        ynt = {dd: S0(f"ynt{dd}", [128, 2048], BF16) for dd in ("fwd", "bwd")}
        bynt = {dd: B() for dd in ("fwd", "bwd")}
        yfm = {dd: S0(f"yfm{dd}", [128, 16 * 512], BF16) for dd in ("fwd", "bwd")}
        byfm = {dd: B() for dd in ("fwd", "bwd")}
        ofm = {dd: S0(f"ofm{dd}", [128, 8 * 512], BF16) for dd in ("fwd", "bwd")}
        bofm = {dd: B() for dd in ("fwd", "bwd")}
        xres = S0("xres", [128, 1024])
        bxres = B()
        x1ev = S0("x1ev", [128, 1024])
        bx1ev = B()
        pf = [P0(f"pf{i}", [128, 1024]) for i in range(3)]
        bpf = [B() for _ in range(3)]
        pb16 = P0("pb16", [128, 1024], BF16)
        bpb16 = B()
        bscr_x1 = B("w_x1")

        for i in range(NT):
            r0 = i * T
            for k in range(4):
                tok0 = r0 + k * 128
                for dd in ("fwd", "bwd"):
                    scr_yn = g["scr_ynf"] if dd == "fwd" else g["scr_ynbw"]
                    sch.dma("pool", ynt[dd][:, :], scr_yn[tok0:tok0 + 128, :], writes=[bynt[dd]])
                    for half in range(2):
                        for cc in range(8):
                            sch.do("pe", lambda h, dd=dd, half=half, cc=cc: h.transpose(
                                pb16[:, cc * 128:(cc + 1) * 128],
                                ynt[dd][:, (half * 8 + cc) * 128:(half * 8 + cc) * 128 + 128],
                                cb["ident_bf"][:, :]),
                                reads=[bynt[dd], C.cbuf["ident_bf"]], writes=[bpb16])
                        sch.do("act", lambda h, dd=dd, half=half, k=k: h.activation(
                            yfm[dd][:, :].rearrange("p (c t) -> p c t", c=16)[:, half * 8:half * 8 + 8,
                                                                              k * 128:(k + 1) * 128],
                            pb16[:, :].rearrange("p (c t) -> p c t", c=8), AF.Copy),
                            reads=[bpb16], writes=[byfm[dd]])
            for dd in ("fwd", "bwd"):
                for m in range(8):
                    pg, bpg = pf[m % 2], bpf[m % 2]
                    mm = None
                    for kc in range(16):
                        mm = sch.do("pe", lambda h, dd=dd, m=m, kc=kc, pg=pg, f=(kc == 0): h.matmul(
                            pg[:, 0:512], wof[:, kc * 1024 + m * 128:kc * 1024 + m * 128 + 128],
                            yfm[dd][:, kc * 512:(kc + 1) * 512], start=f, stop=(kc == 15)),
                            reads=[byfm[dd], bwof], writes=[bpg] if kc == 0 else (), extra=[mm])
                    bpg.writers = [mm]
                    sch.do("act", lambda h, dd=dd, m=m, pg=pg: h.activation(
                        ofm[dd][:, m * 512:(m + 1) * 512], pg[:, 0:512], AF.Copy),
                        reads=[bpg], writes=[bofm[dd]])
            for tc in range(4):
                sch.dma("pool", xres[:, :], g["x_my"][r0 + 3 + tc * 128:r0 + 3 + tc * 128 + 128, :],
                        writes=[bxres])
                for oc in range(2):
                    pg, bpg = pf[2], bpf[2]
                    mm = None
                    for di_, dd in enumerate(("fwd", "bwd")):
                        for kc in range(8):
                            first = (di_ == 0 and kc == 0)
                            mm = sch.do("pe", lambda h, dd=dd, di_=di_, kc=kc, tc=tc, oc=oc, pg=pg, f=first: h.matmul(
                                pg[:, oc * 512:(oc + 1) * 512],
                                ofm[dd][:, kc * 512 + tc * 128:kc * 512 + tc * 128 + 128],
                                wpj[:, (di_ * 8 + kc) * 1024 + oc * 512:(di_ * 8 + kc) * 1024 + (oc + 1) * 512],
                                start=f, stop=False),
                                reads=[bofm[dd], bwpj], writes=[bpg] if first else (), extra=[mm])
                    mm = sch.do("pe", lambda h, oc=oc, pg=pg: h.matmul(
                        pg[:, oc * 512:(oc + 1) * 512], cb["ones1x128b"][:, :],
                        bprow_[0:1, oc * 512:(oc + 1) * 512], start=False, stop=True),
                        reads=[bbprow, C.cbuf["ones1x128b"]], extra=[mm])
                    bpg.writers = [mm]
                    sch.do("dve", lambda h, oc=oc, pg=pg: h.tensor_tensor(
                        x1ev[:, oc * 512:(oc + 1) * 512], pg[:, oc * 512:(oc + 1) * 512],
                        xres[:, oc * 512:(oc + 1) * 512], ALU.add),
                        reads=[bpg, bxres], writes=[bx1ev])
                sch.dma("pool", g["scr_x1"][1 + r0 + tc * 128:1 + r0 + tc * 128 + 128, :], x1ev[:, :],
                        reads=[bx1ev], writes=[bscr_x1])
        C.bx1w = bscr_x1


def _ln_row(C, sch, B, row_io, brow, wrk, bwrk):
    sch.do("dve", lambda h: h.reduce_sum(wrk[0:1, 0:1], row_io[0:1, :], axis=AX),
           reads=[brow], writes=[bwrk], strict=True)
    sch.do("act", lambda h: h.activation(wrk[0:1, 4:5], row_io[0:1, :], AF.Square,
                                         accum_out=wrk[0:1, 1:2]),
           reads=[brow], writes=[bwrk], strict=True) if False else None
    sch.do("act", lambda h: h.activation(wrk2_glob[C][0:1, :], row_io[0:1, :], AF.Square),
           reads=[brow], writes=[bwrk], strict=True)
    sch.do("dve", lambda h: h.reduce_sum(wrk[0:1, 1:2], wrk2_glob[C][0:1, :], axis=AX),
           reads=[bwrk], writes=[bwrk], strict=True)
    sch.do("dve", lambda h: h.tensor_scalar(wrk[0:1, 0:1], wrk[0:1, 0:1], 1.0 / D, None, ALU.mult),
           reads=[bwrk], writes=[bwrk], strict=True)
    sch.do("dve", lambda h: h.tensor_scalar(wrk[0:1, 1:2], wrk[0:1, 1:2], 1.0 / D, None, ALU.mult),
           reads=[bwrk], writes=[bwrk], strict=True)
    sch.do("dve", lambda h: h.tensor_tensor(wrk[0:1, 2:3], wrk[0:1, 0:1], wrk[0:1, 0:1], ALU.mult),
           reads=[bwrk], writes=[bwrk], strict=True)
    sch.do("dve", lambda h: h.tensor_tensor(wrk[0:1, 1:2], wrk[0:1, 1:2], wrk[0:1, 2:3], ALU.subtract),
           reads=[bwrk], writes=[bwrk], strict=True)
    sch.do("act", lambda h: h.activation(wrk[0:1, 1:2], wrk[0:1, 1:2], AF.Sqrt, bias=C.fc[0:1, 0:1]),
           reads=[bwrk, C.bfc], writes=[bwrk], strict=True)
    sch.do("dve", lambda h: h.reciprocal(wrk[0:1, 1:2], wrk[0:1, 1:2]),
           reads=[bwrk], writes=[bwrk], strict=True)
    sch.do("dve", lambda h: h.tensor_scalar(row_io[0:1, :], row_io[0:1, :], wrk[0:1, 0:1], wrk[0:1, 1:2],
                                            ALU.subtract, ALU.mult),
           reads=[bwrk], writes=[brow], strict=True)


wrk2_glob = {}


def attn_phase(C, l):
    sch, S, B, cb = C.sch, C.S, C.B, C.cb
    g = C.g
    nc = C.nc
    s_in = g["scr_x1"] if l == 0 else g["scr_s2"]
    s_out = g["scr_s1"] if l == 0 else g["scr_s3"]
    with ExitStack() as ph:
        def S0(n, s, dt=F32):
            return ph.enter_context(nc.sbuf_tensor(f"a{l}_" + n, list(s), dt))

        def P0(n, s, dt=F32):
            q = 512 if dt == F32 else 1024
            s = list(s)
            s[1] = ((s[1] + q - 1) // q) * q
            return ph.enter_context(nc.psum_tensor(f"a{l}_" + n, s, dt))

        wqkv = S0("wqkv", [128, 8 * 1536], BF16)
        bwqkv = B()
        for kc in range(8):
            sch.dma("pool", wqkv[:, kc * 1536:(kc + 1) * 1536],
                    g[f"wqkv{l}"][kc * 128:(kc + 1) * 128, :], writes=[bwqkv])
        wo = S0("wo", [128, 4 * 1024], BF16)
        bwo = B()
        for kc in range(4):
            sch.dma("pool", wo[:, kc * 1024:(kc + 1) * 1024],
                    g[f"wo{l}"][kc * 128:(kc + 1) * 128, :], writes=[bwo])
        qsc = S0("qsc", [64, 8])
        bqsc = B()
        sch.dma("pool", qsc[:, 0:1], g[f"qs{l}"], writes=[bqsc])
        sch.dma("pool", qsc[:, 1:2], g[f"ks{l}"], writes=[bqsc])
        sch.dma("pool", qsc[0:32, 2:3], g[f"qs{l}"][32:64, 0:1], writes=[bqsc])
        sch.dma("pool", qsc[32:64, 2:3], g[f"qs{l}"][0:32, 0:1], writes=[bqsc])
        sch.dma("pool", qsc[0:32, 3:4], g[f"ks{l}"][32:64, 0:1], writes=[bqsc])
        sch.dma("pool", qsc[32:64, 3:4], g[f"ks{l}"][0:32, 0:1], writes=[bqsc])
        sch.do("dve", lambda h: h.tensor_scalar(qsc[:, 4:5], qsc[:, 0:1], -8.0, None, ALU.mult),
               reads=[bqsc], writes=[bqsc])
        sch.do("dve", lambda h: h.tensor_scalar(qsc[:, 5:6], qsc[:, 1:2], -1.0, None, ALU.mult),
               reads=[bqsc], writes=[bqsc])
        sch.do("dve", lambda h: h.tensor_scalar(qsc[:, 6:7], qsc[:, 2:3], cb["signcol128"][0:64, 0:1], -8.0,
                                                ALU.mult, ALU.mult), reads=[bqsc, C.cbuf["signcol128"]], writes=[bqsc])
        sch.do("dve", lambda h: h.tensor_scalar(qsc[:, 7:8], qsc[:, 3:4], cb["signcol128"][0:64, 0:1], -1.0,
                                                ALU.mult, ALU.mult), reads=[bqsc, C.cbuf["signcol128"]], writes=[bqsc])

        st = [S0(f"st{i}", [128, 1024]) for i in range(4)]
        bst = [B() for _ in range(4)]
        stb = S0("stb", [128, 1024], BF16)
        bstb = B()
        ufm = S0("ufm", [128, 8 * 516], BF16)
        bufm = B()
        trow = S0("trow", [1, 3 * 512])
        btrow = B()
        modw = S0("modw", [64, 4 * 512])
        bmodw = B()
        tabs = S0("tabs", [128, 4 * 512], BF16)
        btabs = B()
        xst = S0("xst", [64, 2 * 512])
        bxst = B()
        sqt = S0("sqt", [128, 1024], BF16)
        bsqt = B()
        nrm = S0("nrm", [2, 6 * 512])
        bnrm = B()
        nrmb = S0("nrmb", [2, 512], BF16)
        bnrmb = B()
        qn = S0("qn", [128, 512], BF16)
        bqn = B()
        qsg = S0("qsg", [128, 512], BF16)
        bqsg = B()
        qp = S0("qp", [64, 8 * 512], BF16)
        bqp = B()
        kring = [S0(f"kring{m}", [64, 8 * 128], BF16) for m in range(8)]
        bkring = [B() for _ in range(8)]
        vring = S0("vring", [128, 8 * 512], BF16)
        bvring = B()
        pt = S0("pt", [128, 5 * 128], BF16)
        bpt = B()
        psb = S0("psb", [128, 640], BF16)
        bpsb = B()
        red = S0("red", [128, 4])
        bred = B()
        ofm = S0("ofm", [128, 4 * 512], BF16)
        bofm = B()
        souts = S0("souts", [128, 1024])
        bsouts = B()
        urow = S0("urow", [1, 1024])
        burow = B()
        uwrk = S0("uwrk", [1, 16])
        buwrk = B()
        wrk2 = S0("wrk2", [1, 1024])
        wrk2_glob[C] = wrk2

        pqk = P0("pqk", [128, 1024])
        bpqk = B()
        psc = P0("psc", [128, 1024])
        bpsc = B()
        po = P0("po", [64, 1024])
        bpo = B()
        pb16 = P0("pb16", [128, 1024], BF16)
        bpb16 = B()
        pro = P0("pro", [64, 512])
        bpro = B()

        bsout_w = B("sout")
        bsndk = B()
        bsndv = B()
        bsndr = B()
        brcvk = B()

        def build_tables(r0):
            sch.dma("pool", modw[:, 0:512], g["m1g"][:, r0:r0 + 512], writes=[bmodw])
            sch.dma("pool", modw[:, 1024:1536], g["m2g"][:, r0:r0 + 512], writes=[bmodw])
            sch.do("dve", lambda h: h.tensor_scalar(trow[0:1, 0:512], cb["iota512"][0:1, :],
                                                    cb["flags"][0:1, 2:3], float(r0), ALU.add, ALU.add),
                   reads=[C.cbuf["iota512"], C.cbuf["flags"]], writes=[btrow], strict=True)
            sch.do("dve", lambda h: h.tensor_scalar(trow[0:1, 512:1024], trow[0:1, 0:512],
                                                    -4096.0, 1.0 / 256.0, ALU.add, ALU.mult),
                   reads=[btrow], writes=[btrow], strict=True)
            sch.do("act", lambda h: h.activation(modw[:, 0:512], modw[:, 0:512], AF.Sin),
                   reads=[bmodw], writes=[bmodw], strict=True)
            sch.do("act", lambda h: h.activation(modw[:, 1024:1536], modw[:, 1024:1536], AF.Sin),
                   reads=[bmodw], writes=[bmodw], strict=True)
            f2 = sch.do("pe", lambda h: h.matmul(pro[:, 0:512], cb["lnbase_row"][0:1, :], trow[0:1, 512:1024],
                                                 start=True, stop=True),
                        reads=[btrow, C.cbuf["lnbase_row"]], writes=[bpro])
            sch.do("act", lambda h: h.activation(xst[:, 0:512], pro[:, 0:512], AF.Exp),
                   reads=[bpro], writes=[bxst], strict=True)
            sch.do("act", lambda h: h.activation(xst[:, 512:1024], pro[:, 0:512], AF.Exp, scale=-1.0),
                   reads=[bpro], writes=[bxst], strict=True)
            sch.do("dve", lambda h: h.scalar_tensor_tensor(tabs[0:64, 0:512], modw[:, 1024:1536], qsc[:, 4:5],
                                                           xst[:, 0:512], ALU.mult, ALU.mult),
                   reads=[bmodw, bqsc, bxst], writes=[btabs], strict=True)
            sch.do("dve", lambda h: h.scalar_tensor_tensor(tabs[0:64, 512:1024], modw[:, 0:512], qsc[:, 6:7],
                                                           xst[:, 0:512], ALU.mult, ALU.mult),
                   reads=[bmodw, bqsc, bxst], writes=[btabs], strict=True)
            sch.do("dve", lambda h: h.scalar_tensor_tensor(tabs[0:64, 1024:1536], modw[:, 1024:1536], qsc[:, 5:6],
                                                           xst[:, 512:1024], ALU.mult, ALU.mult),
                   reads=[bmodw, bqsc, bxst], writes=[btabs], strict=True)
            sch.do("dve", lambda h: h.scalar_tensor_tensor(tabs[0:64, 1536:2048], modw[:, 0:512], qsc[:, 7:8],
                                                           xst[:, 512:1024], ALU.mult, ALU.mult),
                   reads=[bmodw, bqsc, bxst], writes=[btabs], strict=True)
            sch.do("dve", lambda h: h.tensor_copy(tabs[64:128, :], tabs[0:64, :]),
                   reads=[btabs], writes=[btabs], strict=True)

        def load_ln_transpose(r0, tile_first):
            for tc in range(4):
                rr = 1 + r0 + tc * 128
                sch.dma("pool", st[tc][:, :], s_in[rr:rr + 128, :], writes=[bst[tc]])
                if l == 0:
                    sch.do("act", lambda h, tc=tc: h.activation(sqt[:, :], st[tc][:, :], AF.Square),
                           reads=[bst[tc]], writes=[bsqt], strict=True)
                    sch.do("dve", lambda h, tc=tc: h.reduce_sum(red[:, 0:1], st[tc][:, :], axis=AX),
                           reads=[bst[tc]], writes=[bred], strict=True)
                    sch.do("dve", lambda h: h.reduce_sum(red[:, 1:2], sqt[:, :], axis=AX),
                           reads=[bsqt], writes=[bred], strict=True)
                    sch.do("dve", lambda h: h.tensor_scalar(red[:, 0:1], red[:, 0:1], 1.0 / D, None, ALU.mult),
                           reads=[bred], writes=[bred], strict=True)
                    sch.do("dve", lambda h: h.tensor_scalar(red[:, 1:2], red[:, 1:2], 1.0 / D, None, ALU.mult),
                           reads=[bred], writes=[bred], strict=True)
                    sch.do("dve", lambda h: h.tensor_tensor(red[:, 2:3], red[:, 0:1], red[:, 0:1], ALU.mult),
                           reads=[bred], writes=[bred], strict=True)
                    sch.do("dve", lambda h: h.tensor_tensor(red[:, 1:2], red[:, 1:2], red[:, 2:3], ALU.subtract),
                           reads=[bred], writes=[bred], strict=True)
                    sch.do("act", lambda h: h.activation(red[:, 1:2], red[:, 1:2], AF.Sqrt, bias=C.fc[:, 0:1]),
                           reads=[bred, C.bfc], writes=[bred], strict=True)
                    sch.do("dve", lambda h: h.reciprocal(red[:, 1:2], red[:, 1:2]),
                           reads=[bred], writes=[bred], strict=True)
                    sch.do("dve", lambda h, tc=tc: h.tensor_scalar(st[tc][:, :], st[tc][:, :], red[:, 0:1],
                                                                   red[:, 1:2], ALU.subtract, ALU.mult),
                           reads=[bred, bst[tc]], writes=[bst[tc]], strict=True)
                sch.do("dve", lambda h, tc=tc: h.tensor_copy(stb[:, :], st[tc][:, :]),
                       reads=[bst[tc]], writes=[bstb], strict=True)
                for fc in range(8):
                    sch.do("pe", lambda h, fc=fc: h.transpose(
                        pb16[:, fc * 128:(fc + 1) * 128], stb[:, fc * 128:(fc + 1) * 128], cb["ident_bf"][:, :]),
                        reads=[bstb, C.cbuf["ident_bf"]], writes=[bpb16])
                sch.do("act", lambda h, tc=tc: h.activation(
                    ufm[:, :].rearrange("p (c q) -> p c q", c=8)[:, :, 1 + tc * 128:1 + tc * 128 + 128],
                    pb16[:, :].rearrange("p (c q) -> p c q", c=8), AF.Copy),
                    reads=[bpb16], writes=[bufm])

        def qkv_and_rot(into_rings, tile_i):
            for m in range(8):
                mm = None
                for kc in range(8):
                    off = 0 if kc >= 4 else 1
                    mm = sch.do("pe", lambda h, m=m, kc=kc, off=off, f=(kc == 0): h.matmul(
                        pqk[:, (m % 2) * 512:(m % 2) * 512 + 512],
                        wqkv[:, kc * 1536 + m * 128:kc * 1536 + m * 128 + 128],
                        ufm[:, kc * 516 + off:kc * 516 + off + 512], start=f, stop=(kc == 7)),
                        reads=[bufm, bwqkv], writes=[bpqk] if kc == 0 else (), extra=[mm])
                bpqk.writers = [mm]
                po_, bpo_ = pqk, bpqk
                pos = (m % 2) * 512
                sch.do("act", lambda h, pos=pos: h.activation(sqt[:, 0:512], po_[:, pos:pos + 512], AF.Square),
                       reads=[bpo_], writes=[bsqt], strict=True)
                nm = sch.do("pe", lambda h: h.matmul(pro[0:2, 0:512], cb["onespair"][:, :], sqt[:, 0:512],
                                                     start=True, stop=True),
                            reads=[bsqt, C.cbuf["onespair"]], writes=[bpro])
                sch.do("act", lambda h: h.activation(nrm[:, 0:512], pro[0:2, 0:512], AF.Sqrt),
                       reads=[bpro], writes=[bnrm], strict=True)
                sch.do("dve", lambda h: h.tensor_scalar(nrm[:, 512:1024], nrm[:, 0:512], 1e-12, None, ALU.max),
                       reads=[bnrm], writes=[bnrm], strict=True)
                sch.do("dve", lambda h: h.reciprocal(nrm[:, 1024:1536], nrm[:, 512:1024]),
                       reads=[bnrm], writes=[bnrm], strict=True)
                sch.do("dve", lambda h: h.tensor_copy(nrmb[:, :], nrm[:, 1024:1536]),
                       reads=[bnrm], writes=[bnrmb], strict=True)
                bc = sch.do("pe", lambda h: h.matmul(psc[:, 0:512], cb["onespairT"][:, :], nrmb[:, :],
                                                     start=True, stop=True),
                            reads=[bnrmb, C.cbuf["onespairT"]], writes=[bpsc])
                sch.do("act", lambda h, pos=pos: h.activation(qsg[:, :], po_[:, pos:pos + 512], AF.Copy),
                       reads=[bpo_], writes=[bqsg], strict=True)
                sch.do("dve", lambda h: h.tensor_tensor(qn[:, :], qsg[:, :], psc[:, 0:512], ALU.mult),
                       reads=[bqsg, bpsc], writes=[bqn], strict=True)
                for qq in range(4):
                    srcq = (qq // 2) * 64 + (1 - (qq % 2)) * 32
                    sch.do("dve", lambda h, qq=qq, srcq=srcq: h.tensor_copy(
                        qsg[qq * 32:(qq + 1) * 32, :], qn[srcq:srcq + 32, :]),
                        reads=[bqn], writes=[bqsg], strict=True)
                tb = 0 if m < 4 else 1024
                dst = qp if m < 4 else None
                sch.do("dve", lambda h, tb=tb: h.tensor_tensor(
                    qn[:, :], qn[:, :], tabs[:, tb:tb + 512], ALU.mult),
                    reads=[bqn, btabs], writes=[bqn], strict=True)
                sch.do("dve", lambda h, tb=tb: h.tensor_tensor(
                    qsg[:, :], qsg[:, :], tabs[:, tb + 512:tb + 1024], ALU.mult),
                    reads=[bqsg, btabs], writes=[bqsg], strict=True)
                if m < 4:
                    for hp2 in range(2):
                        sch.do("dve", lambda h, m=m, hp2=hp2: h.tensor_tensor(
                            qp[:, (2 * m + hp2) * 512:(2 * m + hp2 + 1) * 512],
                            qn[hp2 * 64:(hp2 + 1) * 64, :], qsg[hp2 * 64:(hp2 + 1) * 64, :], ALU.add),
                            reads=[bqn, bqsg], writes=[bqp], strict=True)
                else:
                    mk = m - 4
                    slot = tile_i % 2
                    for hp2 in range(2):
                        hh3 = 2 * mk + hp2
                        sch.do("dve", lambda h, hh3=hh3, hp2=hp2, slot=slot: h.tensor_tensor(
                            kring[hh3][:, slot * 512:(slot + 1) * 512],
                            qn[hp2 * 64:(hp2 + 1) * 64, :], qsg[hp2 * 64:(hp2 + 1) * 64, :], ALU.add),
                            reads=[bqn, bqsg], writes=[bkring[hh3]], strict=True)
            for tc in range(4):
                mm = None
                for kc in range(8):
                    off = 0 if kc >= 4 else 1
                    mm = sch.do("pe", lambda h, tc=tc, kc=kc, off=off, f=(kc == 0): h.matmul(
                        pqk[:, (tc % 2) * 512:(tc % 2) * 512 + 512],
                        ufm[:, kc * 516 + off + tc * 128:kc * 516 + off + tc * 128 + 128],
                        wqkv[:, kc * 1536 + 1024:kc * 1536 + 1536], start=f, stop=(kc == 7)),
                        reads=[bufm, bwqkv], writes=[bpqk] if kc == 0 else (), extra=[mm])
                bpqk.writers = [mm]
                slot = (tile_i * 4 + tc) % 8
                sch.do("act", lambda h, tc=tc, slot=slot: h.activation(
                    vring[:, slot * 512:(slot + 1) * 512], pqk[:, (tc % 2) * 512:(tc % 2) * 512 + 512], AF.Copy),
                    reads=[bpqk], writes=[bvring], strict=True)

        build_tables(7 * 512)
        load_ln_transpose(7 * 512, True)
        sch.dma("pool", urow[0:1, :], s_in[7 * 512:7 * 512 + 1, :], writes=[burow])
        if l == 0:
            _ln_row(C, sch, B, urow, burow, uwrk, buwrk)
        sch.do("dve", lambda h: h.tensor_copy(stb[0:1, :], urow[0:1, :]), reads=[burow], writes=[bstb], strict=True)
        for fc in range(8):
            sch.do("pe", lambda h, fc=fc: h.transpose(pb16[:, fc * 128:fc * 128 + 1],
                                                      stb[0:1, fc * 128:(fc + 1) * 128], cb["ident_bf"][0:1, 0:1]),
                   reads=[bstb, C.cbuf["ident_bf"]], writes=[bpb16])
        sch.do("act", lambda h: h.activation(
            ufm[:, :].rearrange("p (c q) -> p c q", c=8)[:, :, 0:1],
            pb16[:, 0:8 * 128].rearrange("p (c q) -> p c q", c=8)[:, :, 0:1], AF.Copy),
            reads=[bpb16], writes=[bufm])
        qkv_and_rot(True, 1)
        for hh3 in range(8):
            sch.dma("pool", g[f"snd_k{l}"][:, hh3 * 512:(hh3 + 1) * 512],
                    kring[hh3][:, 4 * 128:8 * 128], reads=[bkring[hh3]], writes=[bsndk])
        sch.dma("pool", g[f"snd_v{l}"][:, :].rearrange("(s p) d -> p s d", p=128),
                vring[:, 4 * 512:8 * 512].rearrange("p (s d) -> p s d", s=4), reads=[bvring], writes=[bsndv])
        sch.dma("pool", g[f"snd_row{l}"][0:1, :], s_in[R:R + 1, :], writes=[bsndr])
        cc1 = sch.do("pool", lambda h: h.collective_compute(
            "AllGather", ALU.bypass, replica_groups=[[0, 1], [2, 3], [4, 5], [6, 7]],
            ins=[g[f"snd_k{l}"].opt()], outs=[g[f"rcv_k{l}"].opt()]),
            reads=[bsndk], writes=[brcvk], is_cc=True)
        cc2 = sch.do("pool", lambda h: h.collective_compute(
            "AllGather", ALU.bypass, replica_groups=[[0, 1], [2, 3], [4, 5], [6, 7]],
            ins=[g[f"snd_v{l}"].opt()], outs=[g[f"rcv_v{l}"].opt()]),
            reads=[bsndv], writes=[brcvk], is_cc=True)
        cc3 = sch.do("pool", lambda h: h.collective_compute(
            "AllGather", ALU.bypass, replica_groups=[[0, 1], [2, 3], [4, 5], [6, 7]],
            ins=[g[f"snd_row{l}"].opt()], outs=[g[f"rcv_row{l}"].opt()]),
            reads=[bsndr], writes=[brcvk], is_cc=True)
        for hh3 in range(8):
            sch.dma("pool", kring[hh3][:, 4 * 128:8 * 128],
                    g[f"rcv_k{l}"][0:64, hh3 * 512:(hh3 + 1) * 512],
                    reads=[brcvk], writes=[bkring[hh3]])
            sch.do("dve", lambda h, hh3=hh3: h.tensor_scalar(
                kring[hh3][:, 4 * 128:8 * 128], kring[hh3][:, 4 * 128:8 * 128],
                cb["flags"][0:64, 0:1], None, ALU.mult),
                reads=[C.cbuf["flags"]], writes=[bkring[hh3]], strict=True)
        sch.dma("pool", vring[:, 4 * 512:8 * 512].rearrange("p (s d) -> p s d", s=4),
                g[f"rcv_v{l}"][0:512, :].rearrange("(s p) d -> p s d", p=128),
                reads=[brcvk], writes=[bvring])
        sch.do("dve", lambda h: h.tensor_scalar(vring[:, 4 * 512:8 * 512], vring[:, 4 * 512:8 * 512],
                                                cb["flags"][:, 0:1], None, ALU.mult),
               reads=[C.cbuf["flags"]], writes=[bvring], strict=True)
        sch.dma("pool", urow[0:1, :], g[f"rcv_row{l}"][0:1, :], reads=[brcvk], writes=[burow])
        if l == 0:
            _ln_row(C, sch, B, urow, burow, uwrk, buwrk)
        sch.do("dve", lambda h: h.tensor_scalar(urow[0:1, :], urow[0:1, :], cb["flags"][0:1, 0:1], None, ALU.mult),
               reads=[C.cbuf["flags"]], writes=[burow], strict=True)
        sch.do("dve", lambda h: h.tensor_copy(stb[0:1, :], urow[0:1, :]), reads=[burow], writes=[bstb], strict=True)
        for fc in range(8):
            sch.do("pe", lambda h, fc=fc: h.transpose(pb16[:, fc * 128:fc * 128 + 1],
                                                      stb[0:1, fc * 128:(fc + 1) * 128], cb["ident_bf"][0:1, 0:1]),
                   reads=[bstb, C.cbuf["ident_bf"]], writes=[bpb16])
        stash0 = sch.do("act", lambda h: h.activation(
            ufm[:, :].rearrange("p (c q) -> p c q", c=8)[:, :, 0:1],
            pb16[:, 0:8 * 128].rearrange("p (c q) -> p c q", c=8)[:, :, 0:1], AF.Copy),
            reads=[bpb16], writes=[bufm])

        for i in range(NT):
            r0 = i * T
            build_tables(r0)
            load_ln_transpose(r0, i == 0)
            qkv_and_rot(True, i)
            for m in range(4):
                for hp in range(2):
                    hh = m * 2 + hp
                    for qc in range(4):
                        gchunk = i * 4 + qc
                        mm = None
                        first_in_bank = [True, True]
                        for j in range(5):
                            kv_g = gchunk - 4 + j
                            slot = kv_g % 8
                            col = j * 128
                            bank = 1 if col >= 512 else 0
                            f = first_in_bank[bank]
                            first_in_bank[bank] = False
                            mm = sch.do("pe", lambda h, hh=hh, qc=qc, slot=slot, col=col, f=f: h.matmul(
                                psc[:, col:col + 128],
                                qp[:, hh * 512 + qc * 128:hh * 512 + qc * 128 + 128],
                                kring[hh][:, slot * 128:(slot + 1) * 128],
                                start=f, stop=False),
                                reads=[bqp, bkring[hh]], writes=[bpsc] if (f and bank == 0) else (), extra=[mm])
                        mm = sch.do("pe", lambda h: h.matmul(
                            psc[:, 0:128], cb["negbig_ident"][:, :], cb["amask_far"][:, :], start=False, stop=False),
                            reads=[C.cbuf["negbig_ident"], C.cbuf["amask_far"]], extra=[mm])
                        for j in range(5):
                            if gchunk - 4 + j < 0:
                                mm = sch.do("pe", lambda h, j=j: h.matmul(
                                    psc[:, j * 128:(j + 1) * 128], cb["negbig_ident"][:, :], cb["am_mid0"][:, :],
                                    start=False, stop=False), reads=[C.cbuf["negbig_ident"], C.cbuf["am_mid0"]],
                                    extra=[mm])
                        mm = sch.do("pe", lambda h: h.matmul(
                            psc[:, 0:128], cb["negbig_ident"][:, :], cb["am_mid0"][:, :], start=False, stop=True),
                            reads=[C.cbuf["negbig_ident"], C.cbuf["am_mid0"]], extra=[mm]) if False else mm
                        mm = sch.do("pe", lambda h: h.matmul(
                            psc[:, 512:640], cb["negbig_ident"][:, :], cb["amask_diag"][:, :], start=False, stop=True),
                            reads=[C.cbuf["negbig_ident"], C.cbuf["amask_diag"]], extra=[mm])
                        bpsc.writers = [mm]
                        sch.do("dve", lambda h: h.reduce_max(red[:, 0:1], psc[:, 0:640], axis=AX, negate=True),
                               reads=[bpsc], writes=[bred], strict=True)
                        sch.do("act", lambda h: h.activation(psb[:, :], psc[:, 0:640], AF.Exp, bias=red[:, 0:1]),
                               reads=[bpsc, bred], writes=[bpsb], strict=True)
                        if DEBUG_TAPS and l == 0 and i == 0 and m == 0 and hp == 0 and qc == 1:
                            dbgb = B("dbgb")
                            sch.dma("pool", g["scr_dbg2"][:, 8448:9088], psb[:, 0:640], reads=[bpsb], writes=[dbgb])
                            sch.dma("pool", g["scr_dbg2"][:, 9088:9092], red[:, 0:4], reads=[bred], writes=[dbgb], slow=True)
                        sch.do("dve", lambda h: h.reduce_sum(red[:, 1:2], psb[:, :], axis=AX),
                               reads=[bpsb], writes=[bred], strict=True)
                        sch.do("dve", lambda h: h.reciprocal(red[:, 2:3], red[:, 1:2]),
                               reads=[bred], writes=[bred], strict=True)
                        sch.do("dve", lambda h: h.tensor_scalar(psb[:, :], psb[:, :], red[:, 2:3], None, ALU.mult),
                               reads=[bred], writes=[bpsb], strict=True)
                        if DEBUG_TAPS and l == 0 and i == 0 and m == 0 and hp == 0 and qc == 1:
                            dbga = B("dbga")
                            sch.dma("pool", g["scr_dbg2"][0:64, 0:2048], qp[0:64, 0:2048], reads=[bqp], writes=[dbga])
                            sch.dma("pool", g["scr_dbg2"][0:64, 2048:3072], kring[0][:, :], reads=[bkring[0]], writes=[dbga])
                            sch.dma("pool", g["scr_dbg2"][:, 3072:3712], psb[:, 0:640], reads=[bpsb], writes=[dbga])
                            sch.dma("pool", g["scr_dbg2"][:, 4096:6144], ufm[:, 0:2048], reads=[bufm], writes=[dbga])
                            sch.dma("pool", g["scr_dbg2"][:, 8192:8320], cb["am_mid0"][:, :],
                                    reads=[C.cbuf["am_mid0"]], writes=[dbga])
                            sch.dma("pool", g["scr_dbg2"][:, 8320:8448], cb["negbig_ident"][:, :],
                                    reads=[C.cbuf["negbig_ident"]], writes=[dbga])
                        for j in range(5):
                            sch.do("pe", lambda h, j=j: h.transpose(
                                pb16[:, j * 128:(j + 1) * 128], psb[:, j * 128:(j + 1) * 128], cb["ident_bf"][:, :]),
                                reads=[bpsb, C.cbuf["ident_bf"]], writes=[bpb16])
                        sch.do("act", lambda h: h.activation(pt[:, 0:640], pb16[:, 0:640], AF.Copy),
                               reads=[bpb16], writes=[bpt], strict=True)
                        mm = None
                        for j in range(5):
                            kv_g = gchunk - 4 + j
                            slot = kv_g % 8
                            st_flag = (qc == 0 and j == 0)
                            mm = sch.do("pe", lambda h, hp=hp, hh=hh, qc=qc, slot=slot, j=j, sf=st_flag: h.matmul(
                                po[0:64, hp * 512 + qc * 128:hp * 512 + (qc + 1) * 128],
                                vring[:, slot * 512 + hh * 64:slot * 512 + hh * 64 + 64],
                                pt[:, j * 128:(j + 1) * 128],
                                start=sf, stop=(j == 4)),
                                reads=[bvring, bpt], writes=[bpo] if (j == 0) else (), extra=[mm])
                        bpo.writers = [mm]
                sch.do("act", lambda h, m=m: h.activation(ofm[0:64, m * 512:(m + 1) * 512], po[0:64, 0:512], AF.Copy),
                       reads=[bpo], writes=[bofm], strict=True)
                sch.do("act", lambda h, m=m: h.activation(ofm[64:128, m * 512:(m + 1) * 512], po[0:64, 512:1024], AF.Copy),
                       reads=[bpo], writes=[bofm], strict=True)
            if DEBUG_TAPS and l == 0 and i == 0:
                dbgo = B("dbgo")
                sch.dma("pool", g["scr_dbg2"][:, 6144:8192], ofm[:, 0:2048], reads=[bofm], writes=[dbgo])
            for tc in range(4):
                for oc in range(2):
                    mm = None
                    for kc in range(4):
                        mm = sch.do("pe", lambda h, tc=tc, oc=oc, kc=kc, f=(kc == 0): h.matmul(
                            psc[:, oc * 512:(oc + 1) * 512],
                            ofm[:, kc * 512 + tc * 128:kc * 512 + tc * 128 + 128],
                            wo[:, kc * 1024 + oc * 512:kc * 1024 + (oc + 1) * 512], start=f, stop=(kc == 3)),
                            reads=[bofm, bwo], writes=[bpsc] if kc == 0 else (), extra=[mm])
                    bpsc.writers = [mm]
                    sch.do("dve", lambda h, tc=tc, oc=oc: h.tensor_tensor(
                        souts[:, oc * 512:(oc + 1) * 512], psc[:, oc * 512:(oc + 1) * 512],
                        st[tc][:, oc * 512:(oc + 1) * 512], ALU.add),
                        reads=[bpsc, bst[tc]], writes=[bsouts], strict=True)
                sch.dma("pool", s_out[1 + r0 + tc * 128:1 + r0 + tc * 128 + 128, :], souts[:, :],
                        reads=[bsouts], writes=[bsout_w])
            sch.do("dve", lambda h: h.tensor_copy(
                ufm[:, :].rearrange("p (c q) -> p c q", c=8)[:, :, 0:1],
                ufm[:, :].rearrange("p (c q) -> p c q", c=8)[:, :, 512:513]),
                reads=[bufm], writes=[bufm], strict=True)
        C.bstream_w = bsout_w


def mlp_phase(C, l):
    sch, S, B, cb = C.sch, C.S, C.B, C.cb
    g = C.g
    nc = C.nc
    s_in = g["scr_s1"] if l == 0 else g["scr_s3"]
    s_out = g["scr_s2"] if l == 0 else None
    with ExitStack() as ph:
        def S0(n, s, dt=F32):
            return ph.enter_context(nc.sbuf_tensor(f"m{l}_" + n, list(s), dt))

        def P0(n, s, dt=F32):
            q = 512 if dt == F32 else 1024
            s = list(s)
            s[1] = ((s[1] + q - 1) // q) * q
            return ph.enter_context(nc.psum_tensor(f"m{l}_" + n, s, dt))

        w1 = S0("w1", [128, 8 * H2P], BF16)
        bw1 = B()
        for kc in range(8):
            sch.dma("pool", w1[:, kc * H2P:(kc + 1) * H2P], g[f"w1p{l}"][kc * 128:(kc + 1) * 128, :], writes=[bw1])
        w2 = S0("w2", [128, 22 * 1024], BF16)
        bw2 = B()
        for kc in range(22):
            kw = 128 if kc < 21 else 64
            sch.dma("pool", w2[0:kw, kc * 1024:(kc + 1) * 1024],
                    g[f"w2p{l}"][kc * 128:kc * 128 + kw, :], writes=[bw2])
        b1c = S0("b1c", [128, 43])
        bb1c = B()
        for m in range(43):
            sch.dma("pool", b1c[:, m:m + 1], g[f"b1p{l}"][m * 128:(m + 1) * 128, 0:1], writes=[bb1c])
        lgb = S0("lgb", [128, 16])
        blgb = B()
        for c2 in range(8):
            sch.dma("pool", lgb[:, c2:c2 + 1], g[f"lng{l}"][c2 * 128:(c2 + 1) * 128, 0:1], writes=[blgb])
            sch.dma("pool", lgb[:, 8 + c2:9 + c2], g[f"lnb{l}"][c2 * 128:(c2 + 1) * 128, 0:1], writes=[blgb])
        b2r = S0("b2r", [1, 1024], BF16)
        bb2r = B()
        sch.dma("pool", b2r[0:1, :], g[f"b2{l}"].rearrange("d one -> one d"), writes=[bb2r])

        st = [S0(f"st{i}", [128, 1024]) for i in range(4)]
        bst = [B() for _ in range(4)]
        if l == 1:
            pass
        stb = S0("stb", [128, 1024], BF16)
        bstb = B()
        sfm = S0("sfm", [128, 8 * 516], BF16)
        bsfm = B()
        hmb = S0("hmb", [128, 8 * 512], BF16)
        bhmb = B()
        if l == 1:
            x1t = hmb[:, 0:2048].bitcast(F32)
            bx1t = bhmb
        nrow = S0("nrow", [1, 2 * 512])
        bnrow = B()
        nrowb = S0("nrowb", [1, 2 * 512], BF16)
        bnrowb = B()
        ffp = S0("ffp", [128, 22 * 512], BF16)
        bffp = B()
        aev = S0("aev", [64, 2 * 512], BF16)
        baev = B()
        sout = S0("sout", [128, 1024])
        bsout = B()
        urow = sout
        burow = bsout

        pmm = P0("pmm", [128, 1024])
        bpmm = B()
        pst = P0("pst", [1, 1024])
        bpst = B()
        pbc = P0("pbc", [128, 1024])
        bpbc = B()
        pb16 = P0("pb16", [128, 1024], BF16)
        bpb16 = B()

        bsout_w = B("sout")
        bsndr = B()
        brcvr = B()

        sch.dma("pool", g[f"snd_mrow{l}"][0:1, :], s_in[R:R + 1, :], writes=[bsndr])
        sch.do("pool", lambda h: h.collective_compute(
            "AllGather", ALU.bypass, replica_groups=[[0, 1], [2, 3], [4, 5], [6, 7]],
            ins=[g[f"snd_mrow{l}"].opt()], outs=[g[f"rcv_mrow{l}"].opt()]),
            reads=[bsndr], writes=[brcvr], is_cc=True)
        sch.dma("pool", urow[0:1, :], g[f"rcv_mrow{l}"][0:1, :], reads=[brcvr], writes=[burow])
        sch.do("dve", lambda h: h.tensor_scalar(urow[0:1, :], urow[0:1, :], cb["flags"][0:1, 0:1], None, ALU.mult),
               reads=[C.cbuf["flags"]], writes=[burow], strict=True)
        sch.do("dve", lambda h: h.tensor_copy(stb[0:1, :], urow[0:1, :]), reads=[burow], writes=[bstb], strict=True)
        for fc in range(8):
            sch.do("pe", lambda h, fc=fc: h.transpose(pb16[:, fc * 128:fc * 128 + 1],
                                                      stb[0:1, fc * 128:(fc + 1) * 128], cb["ident_bf"][0:1, 0:1]),
                   reads=[bstb, C.cbuf["ident_bf"]], writes=[bpb16])
        sch.do("act", lambda h: h.activation(
            sfm[:, :].rearrange("p (c q) -> p c q", c=8)[:, :, 0:1],
            pb16[:, 0:8 * 128].rearrange("p (c q) -> p c q", c=8)[:, :, 0:1], AF.Copy),
            reads=[bpb16], writes=[bsfm])

        for i in range(NT):
            r0 = i * T
            for tc in range(4):
                rr = 1 + r0 + tc * 128
                sch.dma("pool", st[tc][:, :], s_in[rr:rr + 128, :], writes=[bst[tc]])
                sch.do("dve", lambda h, tc=tc: h.tensor_copy(stb[:, :], st[tc][:, :]),
                       reads=[bst[tc]], writes=[bstb], strict=True)
                for fc in range(8):
                    sch.do("pe", lambda h, fc=fc: h.transpose(
                        pb16[:, fc * 128:(fc + 1) * 128], stb[:, fc * 128:(fc + 1) * 128], cb["ident_bf"][:, :]),
                        reads=[bstb, C.cbuf["ident_bf"]], writes=[bpb16])
                sch.do("act", lambda h, tc=tc: h.activation(
                    sfm[:, :].rearrange("p (c q) -> p c q", c=8)[:, :, 1 + tc * 128:1 + tc * 128 + 128],
                    pb16[:, :].rearrange("p (c q) -> p c q", c=8), AF.Copy),
                    reads=[bpb16], writes=[bsfm])
            s1g = None
            for kc in range(8):
                off = 0 if kc >= 4 else 1
                s1g = sch.do("pe", lambda h, kc=kc, off=off, f=(kc == 0): h.matmul(
                    pst[0:1, 0:512], cb["onescol_b"][:, :], sfm[:, kc * 516 + off:kc * 516 + off + 512],
                    start=f, stop=(kc == 7)), reads=[bsfm, C.cbuf["onescol_b"]],
                    writes=[bpst] if kc == 0 else (), extra=[s1g])
            s2g = s1g
            for kc in range(8):
                off = 0 if kc >= 4 else 1
                sch.do("act", lambda h, kc=kc, off=off: h.activation(
                    stb[:, 0:512], sfm[:, kc * 516 + off:kc * 516 + off + 512], AF.Square),
                    reads=[bsfm], writes=[bstb], strict=True)
                s2g = sch.do("pe", lambda h, kc=kc, f=(kc == 0): h.matmul(
                    pst[0:1, 512:1024], cb["onescol_b"][:, :], stb[:, 0:512],
                    start=f, stop=(kc == 7)), reads=[bstb, C.cbuf["onescol_b"]],
                    writes=[bpst] if kc == 0 else (), extra=[s2g])
            bpst.writers = [s2g]
            sch.do("dve", lambda h: h.tensor_scalar(nrow[0:1, 0:512], pst[0:1, 0:512], 1.0 / D, None, ALU.mult),
                   reads=[bpst], writes=[bnrow], strict=True)
            sch.do("dve", lambda h: h.tensor_scalar(nrow[0:1, 512:1024], pst[0:1, 512:1024], 1.0 / D, None, ALU.mult),
                   reads=[bpst], writes=[bnrow], strict=True)
            sch.do("dve", lambda h: h.tensor_tensor(nrowb[0:1, 0:512], nrow[0:1, 0:512], nrow[0:1, 0:512], ALU.mult),
                   reads=[bnrow], writes=[bnrowb], strict=True)
            sch.do("dve", lambda h: h.tensor_tensor(nrow[0:1, 512:1024], nrow[0:1, 512:1024], nrowb[0:1, 0:512],
                                                    ALU.subtract), reads=[bnrow, bnrowb], writes=[bnrow], strict=True)
            sch.do("act", lambda h: h.activation(nrow[0:1, 512:1024], nrow[0:1, 512:1024], AF.Sqrt,
                                                 bias=C.fc[0:1, 0:1]), reads=[bnrow, C.bfc], writes=[bnrow], strict=True)
            sch.do("dve", lambda h: h.reciprocal(nrow[0:1, 512:1024], nrow[0:1, 512:1024]),
                   reads=[bnrow], writes=[bnrow], strict=True)
            sch.do("dve", lambda h: h.tensor_copy(nrowb[0:1, 0:512], nrow[0:1, 0:512]),
                   reads=[bnrow], writes=[bnrowb], strict=True)
            sch.do("dve", lambda h: h.tensor_copy(nrowb[0:1, 512:1024], nrow[0:1, 512:1024]),
                   reads=[bnrow], writes=[bnrowb], strict=True)
            bc1 = sch.do("pe", lambda h: h.matmul(pbc[:, 0:512], cb["ones1x128b"][:, :], nrowb[0:1, 0:512],
                                                  start=True, stop=True),
                         reads=[bnrowb, C.cbuf["ones1x128b"]], writes=[bpbc])
            bc2 = sch.do("pe", lambda h: h.matmul(pbc[:, 512:1024], cb["ones1x128b"][:, :], nrowb[0:1, 512:1024],
                                                  start=True, stop=True),
                         reads=[bnrowb, C.cbuf["ones1x128b"]], writes=[bpbc])
            for kc in range(8):
                off = 0 if kc >= 4 else 1
                sch.do("dve", lambda h, kc=kc, off=off: h.tensor_tensor(
                    stb[:, 0:512], sfm[:, kc * 516 + off:kc * 516 + off + 512], pbc[:, 0:512], ALU.subtract),
                    reads=[bsfm, bpbc], writes=[bstb], strict=True)
                sch.do("dve", lambda h, kc=kc: h.tensor_tensor(
                    stb[:, 512:1024], stb[:, 0:512], pbc[:, 512:1024], ALU.mult),
                    reads=[bstb, bpbc], writes=[bstb], strict=True)
                sch.do("dve", lambda h, kc=kc: h.tensor_scalar(
                    hmb[:, kc * 512:(kc + 1) * 512], stb[:, 512:1024], lgb[:, kc:kc + 1], lgb[:, 8 + kc:9 + kc],
                    ALU.mult, ALU.add), reads=[bstb, blgb], writes=[bhmb], strict=True)
            for m in range(43):
                mm = None
                for kc in range(8):
                    mm = sch.do("pe", lambda h, m=m, kc=kc, f=(kc == 0): h.matmul(
                        pmm[:, (m % 2) * 512:(m % 2) * 512 + 512],
                        w1[:, kc * H2P + m * 128:kc * H2P + m * 128 + 128],
                        hmb[:, kc * 512:(kc + 1) * 512], start=f, stop=(kc == 7)),
                        reads=[bhmb, bw1], writes=[bpmm] if kc == 0 else (), extra=[mm])
                bpmm.writers = [mm]
                pos = (m % 2) * 512
                sch.do("dve", lambda h, m=m, pos=pos: h.tensor_scalar(
                    aev[:, 0:512], pmm[0:64, pos:pos + 512], b1c[0:64, m:m + 1], None, ALU.add),
                    reads=[bpmm, bb1c], writes=[baev], strict=True)
                sch.do("act", lambda h, m=m, pos=pos: h.activation(
                    aev[:, 512:1024], pmm[64:128, pos:pos + 512], AF.Gelu, bias=b1c[64:128, m:m + 1]),
                    reads=[bpmm, bb1c], writes=[baev], strict=True)
                sch.do("dve", lambda h, m=m: h.tensor_tensor(
                    ffp[(m % 2) * 64:(m % 2) * 64 + 64, (m // 2) * 512:(m // 2) * 512 + 512],
                    aev[:, 0:512], aev[:, 512:1024], ALU.mult),
                    reads=[baev], writes=[bffp], strict=True)
            for tc in range(4):
                if l == 1:
                    sch.dma("pool", x1t[:, :], g["scr_x1"][1 + r0 + tc * 128:1 + r0 + tc * 128 + 128, :],
                            writes=[bx1t])
                for oc in range(2):
                    mm = None
                    for kc in range(22):
                        kw = 128 if kc < 21 else 64
                        mm = sch.do("pe", lambda h, tc=tc, oc=oc, kc=kc, kw=kw, f=(kc == 0): h.matmul(
                            pmm[:, oc * 512:(oc + 1) * 512],
                            ffp[0:kw, kc * 512 + tc * 128:kc * 512 + tc * 128 + 128],
                            w2[0:kw, kc * 1024 + oc * 512:kc * 1024 + (oc + 1) * 512], start=f, stop=False),
                            reads=[bffp, bw2], writes=[bpmm] if kc == 0 else (), extra=[mm])
                    mm = sch.do("pe", lambda h, oc=oc: h.matmul(
                        pmm[:, oc * 512:(oc + 1) * 512], cb["ones1x128b"][:, :], b2r[0:1, oc * 512:(oc + 1) * 512],
                        start=False, stop=True), reads=[bb2r, C.cbuf["ones1x128b"]], extra=[mm])
                    bpmm.writers = [mm]
                    sch.do("dve", lambda h, tc=tc, oc=oc: h.tensor_tensor(
                        sout[:, oc * 512:(oc + 1) * 512], pmm[:, oc * 512:(oc + 1) * 512],
                        st[tc][:, oc * 512:(oc + 1) * 512], ALU.add),
                        reads=[bpmm, bst[tc]], writes=[bsout], strict=True)
                    if l == 1:
                        sch.do("dve", lambda h, tc=tc, oc=oc: h.tensor_tensor(
                            sout[:, oc * 512:(oc + 1) * 512], sout[:, oc * 512:(oc + 1) * 512],
                            x1t[:, oc * 512:(oc + 1) * 512], ALU.add),
                            reads=[bx1t], writes=[bsout], strict=True)
                if l == 0:
                    sch.dma("pool", s_out[1 + r0 + tc * 128:1 + r0 + tc * 128 + 128, :], sout[:, :],
                            reads=[bsout], writes=[bsout_w])
                else:
                    sch.dma("pool", g["y_my"][r0 + tc * 128:r0 + tc * 128 + 128, :], sout[:, :],
                            reads=[bsout], writes=[bsout_w])
            sch.do("dve", lambda h: h.tensor_copy(
                sfm[:, :].rearrange("p (c q) -> p c q", c=8)[:, :, 0:1],
                sfm[:, :].rearrange("p (c q) -> p c q", c=8)[:, :, 512:513]),
                reads=[bsfm], writes=[bsfm], strict=True)
        C.bout_w = bsout_w


def final_wait(C):
    sch, g = C.sch, C.g
    if hasattr(C, "bout_w"):
        sch.do("sp", lambda h: h.nop(), reads=[C.bout_w])
    else:
        b = C.B("fin")
        dep = getattr(C, "bx1w", None)
        for rr in range(0, R, 512):
            sch.dma("pool", g["y_my"][rr:rr + 512, :], g["scr_x1"][1 + rr:1 + rr + 512, :],
                    reads=[dep] if dep else (), writes=[b])
        sch.do("sp", lambda h: h.nop(), reads=[b])


def _prep_weights(inp):
    w1 = np.asarray(inp["L0_ffn_w1"], dtype=np.float32)
    b1 = np.asarray(inp["L0_ffn_b1"], dtype=np.float32)
    w2 = np.asarray(inp["L0_ffn_w2"], dtype=np.float32)
    w1p = np.zeros((2, D, H2P), dtype=np.float32)
    b1p = np.zeros((2, H2P), dtype=np.float32)
    w2p = np.zeros((2, HIDP, D), dtype=np.float32)
    for l in range(2):
        ap = np.zeros((D, HIDP), dtype=np.float32)
        gp = np.zeros((D, HIDP), dtype=np.float32)
        ap[:, :HID] = w1[l][:, :HID]
        gp[:, :HID] = w1[l][:, HID:]
        ab = np.zeros(HIDP, dtype=np.float32)
        gb = np.zeros(HIDP, dtype=np.float32)
        ab[:HID] = b1[l][:HID]
        gb[:HID] = b1[l][HID:]
        for blk in range(HIDP // 64):
            w1p[l][:, blk * 128:blk * 128 + 64] = ap[:, blk * 64:(blk + 1) * 64]
            w1p[l][:, blk * 128 + 64:blk * 128 + 128] = gp[:, blk * 64:(blk + 1) * 64]
            b1p[l][blk * 128:blk * 128 + 64] = ab[blk * 64:(blk + 1) * 64]
            b1p[l][blk * 128 + 64:blk * 128 + 128] = gb[blk * 64:(blk + 1) * 64]
        w2p[l][:HID, :] = w2[l]
    return w1p, b1p, w2p


def kernel(**inp):
    x = np.asarray(inp["L0_attn_x"], dtype=np.float32)
    B_, L, _ = x.shape
    consts = host_consts()
    w1p, b1p, w2p = _prep_weights(inp)
    nc = build_program()
    in_maps = []
    for c in range(8):
        b, hh = c // 2, c % 2
        t0 = hh * (L // 2)
        lo = t0 - 3
        xs = np.zeros((RXP, D), dtype=np.float32)
        s, e = max(lo, 0), min(t0 + R + 3, L)
        xs[s - lo:e - lo] = x[b, s:e]
        m = {
            "x_my": xs,
            "w_inproj": np.asarray(inp["L0_attn_in_proj_w"], np.float32),
            "conv_w": np.asarray(inp["L0_attn_conv_w"], np.float32),
            "conv_b": np.asarray(inp["L0_attn_conv_b"], np.float32)[:, None],
            "dt_bias": np.asarray(inp["L0_attn_dt_bias"], np.float32)[:, None],
            "a_log": np.asarray(inp["L0_attn_A_log"], np.float32)[:, None],
            "d_vec": np.asarray(inp["L0_attn_D"], np.float32)[:, None],
            "norm_w": np.asarray(inp["L0_attn_norm_w"], np.float32)[:, None],
            "w_outproj": np.asarray(inp["L0_attn_out_proj_w"], np.float32),
            "w_proj": np.asarray(inp["L0_attn_proj_w"], np.float32),
            "b_proj": np.asarray(inp["L0_attn_proj_b"], np.float32)[:, None],
        }
        for l in range(2):
            m[f"wqkv{l}"] = np.asarray(inp["L0_ffn_wqkv"], np.float32)[l]
            m[f"qs{l}"] = np.asarray(inp["L0_ffn_q_scale"], np.float32)[l][:, None]
            m[f"ks{l}"] = np.asarray(inp["L0_ffn_k_scale"], np.float32)[l][:, None]
            m[f"wo{l}"] = np.asarray(inp["L0_ffn_wo"], np.float32)[l]
            m[f"lng{l}"] = np.asarray(inp["L0_ffn_ln_g"], np.float32)[l][:, None]
            m[f"lnb{l}"] = np.asarray(inp["L0_ffn_ln_b"], np.float32)[l][:, None]
            m[f"w1p{l}"] = w1p[l]
            m[f"b1p{l}"] = b1p[l][:, None]
            m[f"w2p{l}"] = w2p[l]
            m[f"b2{l}"] = np.asarray(inp["L0_ffn_b2"], np.float32)[l][:, None]
        for k, v in consts.items():
            m[k] = v
        half = np.arange(0, FD, 2, dtype=np.float32)
        invf = 1.0 / 10000.0 ** (half / FD)
        invf = np.concatenate([invf, invf])
        tglob = np.arange(t0, t0 + R, dtype=np.float64)
        fr = tglob[None, :] * invf[:, None].astype(np.float64)
        m["m1g"] = ((fr % (2 * np.pi)) - np.pi).astype(np.float32)
        m["m2g"] = (((fr + np.pi / 2) % (2 * np.pi)) - np.pi).astype(np.float32)
        fl = np.zeros((128, 8), dtype=np.float32)
        fl[:, 0] = float(hh == 1)
        fl[:, 1] = float(hh == 0)
        fl[:, 2] = float(t0)
        m["flags"] = fl
        if hh == 0:
            m["am_far0"] = np.ones((128, 128), dtype=np.float32)
            m["am_mid0"] = np.ones((128, 128), dtype=np.float32)
        else:
            m["am_far0"] = consts["amask_far"]
            m["am_mid0"] = np.zeros((128, 128), dtype=np.float32)
        in_maps.append(m)

    res = run_bass_kernel_spmd(nc, in_maps, core_ids=list(range(8)))
    out = np.zeros((B_, L, D), dtype=np.float32)
    for c in range(8):
        b, hh = c // 2, c % 2
        t0 = hh * (L // 2)
        out[b, t0:t0 + R] = res.results[c]["y_my"]
    if DEBUG_TAPS:
        kernel.last_results = res
    return out
